# version 01
# baseline (speedup 1.0000x reference)
import contextlib
import numpy as np
import ml_dtypes
import concourse.bass as bass
import concourse.mybir as mybir
from concourse.bass_utils import run_bass_kernel_spmd

F32 = mybir.dt.float32
BF16 = mybir.dt.bfloat16
AF = mybir.ActivationFunctionType
ALU = mybir.AluOpType
AX = mybir.AxisListType

TOK = 2048
NT = TOK // 512
NQT = TOK // 128
NCORE = 8
DM = 1024
DCH = DM // 128
NEG = -1.0e30
BF = ml_dtypes.bfloat16


def host_consts():
    c = {}
    c["ident"] = np.eye(128, dtype=np.float32)
    c["onescol"] = np.ones((128, 128), np.float32)
    c["onesrow"] = np.ones((1, 128), np.float32)
    cd = np.zeros((128, 128), np.float32)
    for p_ in range(128):
        cd[p_, p_ + 1:] = NEG
    c["cdiag"] = cd
    cdt = np.zeros((128, 128), np.float32)
    for p_ in range(128):
        cdt[p_, :p_] = NEG
    c["cdiagT"] = cdt
    dhe = 32
    inv = 1.0 / (10000.0 ** (np.arange(0, dhe, 2, dtype=np.float64) / dhe))
    ang = np.arange(TOK, dtype=np.float64)[:, None] * inv[None, :]
    emb = np.concatenate([ang, ang], axis=1)
    cos = (np.cos(emb).T / 8.0).astype(np.float32)
    sin = (np.sin(emb).T / 8.0).astype(np.float32)
    ct = np.full((128, TOK), 1.0 / 8.0, np.float32)
    st = np.zeros((128, TOK), np.float32)
    for b0 in (32, 96):
        ct[b0:b0 + 32] = cos
        st[b0:b0 + 32] = sin
    c["ropecos"] = ct
    c["ropesin"] = st
    R = np.zeros((32, 32), np.float32)
    for dp in range(32):
        if dp % 2 == 0:
            R[dp + 1, dp] = -1.0
        else:
            R[dp - 1, dp] = 1.0
    c["ropeR"] = R
    tin = np.triu(np.ones((128, 128), np.float32))
    tst = np.triu(np.ones((128, 128), np.float32), 1)
    c["triu_inc"] = tin
    c["triu_exc"] = tst
    gm8 = np.zeros((128, 128), np.float32) + NEG
    for a in range(16):
        gm8[a * 8:(a + 1) * 8, a * 8:(a + 1) * 8] = 0.0
    c["gmask8"] = gm8
    wma = np.zeros((128, 384), np.float32)
    for p_ in range(128):
        wma[p_, :p_ + 1] = NEG
        wma[p_, 256 + p_ + 1:] = NEG
    c["wma"] = wma
    wbt = np.zeros((6, 128, 512), np.float32)
    for r in range(6):
        for j in range(4):
            rel = (r - 2) - j
            sub = np.zeros((128, 128), np.float32)
            if rel == 0:
                for p_ in range(128):
                    sub[p_, :p_] = NEG
            elif rel == -1:
                pass
            elif rel == -2:
                for p_ in range(128):
                    sub[p_, p_:] = NEG
            else:
                sub[:] = NEG
            wbt[r, :, j * 128:(j + 1) * 128] = sub
    c["wbt"] = wbt.reshape(6 * 128, 512)
    return c


HC = host_consts()


def blockdiag2(a, b=None):
    b = a if b is None else b
    n, m = a.shape
    out = np.zeros((2 * n, 2 * m), np.float32)
    out[:n, :m] = a
    out[n:, m:] = b
    return np.ascontiguousarray(out)


class P:
    def __init__(self, name):
        self.nc = bass.Bass("TRN2", target_bir_lowering=False)
        self.name = name
        self.stk = contextlib.ExitStack()
        self._sems = {}
        self.cnt = {}

    def dram_in(self, name, shape, dt=F32):
        return self.nc.dram_tensor(name, list(shape), dt, kind="ExternalInput").ap()

    def dram_out(self, name, shape, dt=F32):
        return self.nc.dram_tensor(name, list(shape), dt, kind="ExternalOutput").ap()

    def dram_tmp(self, name, shape, dt=F32):
        return self.nc.dram_tensor(name, list(shape), dt,
                                   kind="ExternalOutput").ap()

    def sb(self, name, shape, dt, stk=None, side="left"):
        return (stk or self.stk).enter_context(
            self.nc.sbuf_tensor(name, list(shape), dt, side=side))

    def ps(self, name, shape, dt=F32, stk=None):
        return (stk or self.stk).enter_context(
            self.nc.psum_tensor(name, list(shape), dt))

    def sem(self, name):
        if name not in self._sems:
            self._sems[name] = self.stk.enter_context(self.nc.semaphore(name))
            self.cnt[name] = 0
        return self._sems[name]

    def snap(self, name):
        return (self.sem(name), self.cnt[name])

    def dma(self, q, out, in_, sname="ld", transpose=False):
        if isinstance(q, bass.BassGpSimd):
            sname = sname + "_g"
        s = self.sem(sname)
        if self.cnt[sname]:
            q.wait_ge(s, self.cnt[sname])
        q.dma_start(out, in_, transpose=transpose).then_inc(s, 16)
        self.cnt[sname] += 16
        return self.cnt[sname]

    def inc(self, instr, sname, n=1):
        instr.then_inc(self.sem(sname), n)
        self.cnt[sname] += n
        return self.cnt[sname]

    def waitall(self, eng, sname="ld"):
        for nm_ in (sname, sname + "_g"):
            if self.cnt.get(nm_, 0):
                eng.wait_ge(self.sem(nm_), self.cnt[nm_])

    def chain(self, eng, instr, name="chn"):
        self.inc(instr, name, 1)
        eng.wait_ge(self.sem(name), self.cnt[name])


def wge(eng, sn, k):
    eng.wait_ge(sn[0], sn[1] + k)


def cast_load(p, g, dst, src, chunks, sname="ld"):
    for k in range(chunks):
        p.dma(g, dst[:, k, :], src[k * 128:(k + 1) * 128, :], sname)


def emit_linear_fm(p, tag, w_sb, rhs_sb, KCH, MCH, evac, n_tiles=NT,
                   pre=None):
    nc = p.nc
    pstk = contextlib.ExitStack()
    psl = [p.ps(f"{tag}ps{i}", [128, 512], F32, stk=pstk) for i in range(4)]
    pe, dv = f"{tag}pe", f"{tag}dv"
    s_pe, s_dv = p.snap(pe), p.snap(dv)
    sched = [(m, n) for m in range(MCH) for n in range(n_tiles)]
    with nc.Block() as b:
        @b.tensor
        def _(te):
            p.waitall(te)
            if pre:
                pre(te)
            for t, (m, n) in enumerate(sched):
                if t >= 4:
                    wge(te, s_dv, t - 3)
                ps = psl[t % 4]
                for k in range(KCH):
                    mm = te.matmul(ps[:], w_sb[:, k, m * 128:(m + 1) * 128],
                                   rhs_sb[:, k, n * 512:(n + 1) * 512],
                                   start=(k == 0), stop=(k == KCH - 1))
                p.inc(mm, pe, 1)

        @b.vector
        def _(v):
            p.waitall(v)
            for t, (m, n) in enumerate(sched):
                wge(v, s_pe, t + 1)
                last = evac(v, m, n, psl[t % 4])
                p.inc(last, dv, 1)
    pstk.close()


def emit_attention(p, tag, H, qkb, kkb, vtm, otm, ablocks, btiles,
                   amask, btmask, stage=99):
    nc = p.nc
    sd = getattr(p, "att_side", "left")
    astk = contextlib.ExitStack()
    nm = p.sb(f"{tag}nm", [128, H * NQT], F32, astk, side=sd)
    sm = p.sb(f"{tag}sm", [128, H * NQT], F32, astk, side=sd)
    rca = p.sb(f"{tag}rca", [128, H * NQT], F32, astk, side=sd)
    junk = p.sb(f"{tag}junk", [128, 2048], BF16, astk, side=sd)
    cnb = p.sb(f"{tag}cnb", [128, 1], F32, astk, side=sd)
    ncnb = p.sb(f"{tag}ncnb", [128, 1], F32, astk, side=sd)
    minb = p.sb(f"{tag}minb", [128, 1], BF16, astk, side=sd)
    c1b = p.sb(f"{tag}c1b", [1, 128], F32, astk, side=sd)
    c1c = p.sb(f"{tag}c1c", [1, 1], BF16, astk, side=sd)
    attb = [p.sb(f"{tag}attb{i}", [128, 512], BF16, astk, side=sd) for i in range(2)]
    idsb, onesr = p.idsb, p.onesr

    def hs(buf, h, t0, w):
        lo = (h % 2) * 64
        return buf[lo:lo + 64, h // 2, t0:t0 + w]

    pstk = contextlib.ExitStack()
    psA = [p.ps(f"{tag}psA{i}", [128, 2048], F32, stk=pstk) for i in range(2)]
    schedA = [(h, qt) for h in range(H) for qt in range(NQT)]
    pA, vA, aA = f"{tag}pA", f"{tag}vA", f"{tag}aA"
    sA_p, sA_v, sA_a = p.snap(pA), p.snap(vA), p.snap(aA)
    with nc.Block() as b:
        @b.tensor
        def _(te):
            p.waitall(te)
            for t, (h, qt) in enumerate(schedA):
                k0, nfull, remw, amlist = ablocks(qt)
                if t >= 2:
                    wge(te, sA_a, t - 1)
                ps = psA[t % 2]
                mm = None
                for j in range(nfull):
                    mm = te.matmul(ps[:, j * 512:(j + 1) * 512],
                                   hs(qkb, h, qt * 128, 128),
                                   hs(kkb, h, k0 * 128 + j * 512, 512),
                                   start=True, stop=True)
                if remw:
                    mm = te.matmul(ps[:, nfull * 512:nfull * 512 + remw],
                                   hs(qkb, h, k0 * 128 + nfull * 512, remw),
                                   start=True, stop=True) if False else \
                         te.matmul(ps[:, nfull * 512:nfull * 512 + remw],
                                   hs(qkb, h, qt * 128, 128),
                                   hs(kkb, h, k0 * 128 + nfull * 512, remw),
                                   start=True, stop=True)
                p.inc(mm, pA, 1)

        @b.vector
        def _(v):
            for t, (h, qt) in enumerate(schedA):
                k0, nfull, remw, amlist = ablocks(qt)
                L = nfull * 512 + remw
                ps = psA[t % 2]
                col = h * NQT + qt
                wge(v, sA_p, t + 1)
                for off, w, mk in amlist:
                    p.chain(v, v.tensor_tensor(ps[:, off:off + w],
                                               ps[:, off:off + w],
                                               mk, ALU.add))
                last = v.reduce_max(nm[:, col:col + 1], ps[:, 0:L],
                                    axis=AX.X, negate=True)
                p.inc(last, vA, 1)

        @b.scalar
        def _(sc):
            for t, (h, qt) in enumerate(schedA):
                k0, nfull, remw, amlist = ablocks(qt)
                L = nfull * 512 + remw
                ps = psA[t % 2]
                col = h * NQT + qt
                wge(sc, sA_v, t + 1)
                if t >= 1:
                    wge(sc, sA_a, t)
                ac = sc.activation(junk[:, 0:L], ps[:, 0:L], AF.Exp,
                                   bias=nm[:, col:col + 1],
                                   accum_out=sm[:, col:col + 1])
                p.inc(ac, aA, 1)
    pstk.close()
    if stage <= 5:
        spill(p, "dbgnm", nm)
        spill(p, "dbgsm", sm)
        p._att_astk = astk
        return

    pstk = contextlib.ExitStack()
    psC = p.ps(f"{tag}psC", [1, 128], BF16, stk=pstk)
    psC2 = p.ps(f"{tag}psC2", [128, 1], F32, stk=pstk)
    with nc.Block() as b:
        @b.vector
        def _(v):
            p.inc(v.tensor_reduce(minb[:], nm[:], axis=AX.X, op=ALU.min),
                  f"{tag}c1", 1)

        @b.tensor
        def _(te):
            te.wait_ge(p.sem(f"{tag}c1"), p.cnt[f"{tag}c1"])
            p.inc(te.transpose(psC[:], minb[:], idsb[:]), f"{tag}c2", 1)

        @b.vector
        def _(v):
            v.wait_ge(p.sem(f"{tag}c2"), p.cnt[f"{tag}c2"])
            p.chain(v, v.tensor_copy(c1b[:], psC[:]))
            p.inc(v.tensor_reduce(c1c[:], c1b[:], axis=AX.X, op=ALU.min),
                  f"{tag}c3", 1)

        @b.tensor
        def _(te):
            te.wait_ge(p.sem(f"{tag}c3"), p.cnt[f"{tag}c3"])
            p.inc(te.matmul(psC2[:], onesr[:], c1c[:], start=True, stop=True),
                  f"{tag}c4", 1)

        @b.vector
        def _(v):
            v.wait_ge(p.sem(f"{tag}c4"), p.cnt[f"{tag}c4"])
            p.chain(v, v.tensor_copy(cnb[:], psC2[:]))
            p.inc(v.tensor_scalar_mul(ncnb[:], cnb[:], -1.0), f"{tag}c5", 1)

        @b.scalar
        def _(sc):
            sc.wait_ge(p.sem(f"{tag}c5"), p.cnt[f"{tag}c5"])
            p.inc(sc.activation(rca[:], nm[:], AF.Exp, bias=ncnb[:, 0:1]),
                  f"{tag}c6", 1)

        @b.vector
        def _(v):
            v.wait_ge(p.sem(f"{tag}c6"), p.cnt[f"{tag}c6"])
            p.chain(v, v.reciprocal(sm[:], sm[:]))
            v.tensor_tensor(rca[:], rca[:], sm[:], ALU.mult)
    pstk.close()
    if stage <= 6:
        spill(p, "dbgrca", rca)
        spill(p, "dbgcnb", cnb)
        p._att_astk = astk
        return

    pstk = contextlib.ExitStack()
    psT = [p.ps(f"{tag}psT{i}", [128, 512], F32, stk=pstk) for i in range(2)]
    psV = [p.ps(f"{tag}psV{i}", [128, 64], F32, stk=pstk) for i in range(4)]
    pB, vB, aB, avd, vE = (f"{tag}pB", f"{tag}vB", f"{tag}aB",
                           f"{tag}avd", f"{tag}vE")
    schedB = []
    for h in range(H):
        for qb in range(NT):
            tl = btiles(qb)
            if tl:
                schedB.append((h, qb, tl))
    cumt = []
    tot = 0
    for _, _, tl in schedB:
        tot += len(tl)
        cumt.append(tot)
    sB_p, sB_v, sB_a = p.snap(pB), p.snap(vB), p.snap(aB)
    sB_avd, sB_vE = p.snap(avd), p.snap(vE)

    with nc.Block() as b:
        @b.tensor
        def _(te):
            gt = 0
            for gi, (h, qb, tiles) in enumerate(schedB):
                if gi >= 1:
                    wge(te, sB_vE, gi)
                for i, (kt, mlist, mset) in enumerate(tiles):
                    if gt >= 2:
                        wge(te, sB_a, gt - 1)
                    mm = te.matmul(psT[gt % 2][:], hs(kkb, h, kt * 128, 128),
                                   hs(qkb, h, qb * 512, 512),
                                   start=True, stop=True)
                    p.inc(mm, pB, 1)
                    if i >= 1:
                        wge(te, sB_a, gt)
                        pkt = tiles[i - 1][0]
                        for j in range(4):
                            mm = te.matmul(
                                psV[j][:],
                                attb[(gt - 1) % 2][:, j * 128:(j + 1) * 128],
                                vtm[:, pkt, h * 64:(h + 1) * 64],
                                start=(i == 1), stop=False)
                        p.inc(mm, avd, 1)
                    gt += 1
                wge(te, sB_a, gt)
                pkt = tiles[-1][0]
                for j in range(4):
                    mm = te.matmul(psV[j][:],
                                   attb[(gt - 1) % 2][:, j * 128:(j + 1) * 128],
                                   vtm[:, pkt, h * 64:(h + 1) * 64],
                                   start=(len(tiles) == 1), stop=True)
                p.inc(mm, avd, 1)

        @b.vector
        def _(v):
            gt = 0
            for gi, (h, qb, tiles) in enumerate(schedB):
                for i, (kt, mlist, mset) in enumerate(tiles):
                    if mlist or mset:
                        wge(v, sB_p, gt + 1)
                        if gt >= 2:
                            wge(v, sB_avd, gt - 1)
                        for off, w, mk in mlist:
                            p.chain(v, v.tensor_tensor(
                                psT[gt % 2][:, off:off + w],
                                psT[gt % 2][:, off:off + w], mk, ALU.add))
                        if mset:
                            v.memset(attb[gt % 2][:, 0:mset], 0.0)
                        p.inc(v.engine_nop(), vB, 1)
                    gt += 1
                wge(v, sB_avd, cumt[gi])
                last = None
                for j in range(4):
                    qt = qb * 4 + j
                    col = h * NQT + qt
                    last = v.tensor_scalar(otm[:, qt, h * 64:(h + 1) * 64],
                                           psV[j][:], rca[:, col:col + 1],
                                           None, ALU.mult)
                p.inc(last, vE, 1)

        @b.scalar
        def _(sc):
            gt = 0
            nvb = 0
            for gi, (h, qb, tiles) in enumerate(schedB):
                for i, (kt, mlist, mset) in enumerate(tiles):
                    wge(sc, sB_p, gt + 1)
                    if mlist or mset:
                        nvb += 1
                        wge(sc, sB_v, nvb)
                        off = mset
                    else:
                        off = 0
                        if gt >= 2:
                            wge(sc, sB_avd, gt - 1)
                    ac = sc.activation(attb[gt % 2][:, off:512],
                                       psT[gt % 2][:, off:512], AF.Exp,
                                       bias=cnb[:, 0:1])
                    p.inc(ac, aB, 1)
                    gt += 1
    pstk.close()
    astk.close()


def make_causal_cbs(cdm_ap, cdtm_ap):
    def ablocks(qt):
        return 0, qt // 4, (qt % 4) * 128 + 128, [(qt * 128, 128, cdm_ap)]

    def btiles(qb):
        out = []
        for kt in range(4 * qb + 4):
            if kt // 4 == qb:
                off = (kt % 4) * 128
                out.append((kt, [(off, 128, cdtm_ap)], off))
            else:
                out.append((kt, [], 0))
        return out
    return ablocks, btiles


def make_full_cbs():
    def ablocks(qt):
        return 0, NT, 0, []

    def btiles(qb):
        return [(kt, [], 0) for kt in range(NQT)]
    return ablocks, btiles


def make_window_cbs(wma_ap, wbt_aps):
    def ablocks(qt):
        k0 = max(0, qt - 2)
        w = (qt - k0 + 1) * 128
        return k0, 0, w, [(0, w, wma_ap[:, (384 - w):384] if w < 384
                           else wma_ap[:])]

    def btiles(qb):
        out = []
        for kt in range(max(0, 4 * qb - 2), 4 * qb + 4):
            r = kt - 4 * qb + 2
            out.append((kt, [(0, 512, wbt_aps[r][:])], 0))
        return out
    return ablocks, btiles


def emit_postln(p, tag, s_hbm, s_bf, sq_bf, out_hbm, out_bf=None, eps=1e-5,
                gb=None):
    nc = p.nc
    lstk = contextlib.ExitStack()
    onescol = p.onescol
    mean_b = p.sb(f"{tag}mean", [128, TOK], F32, lstk)
    rstd_b = p.sb(f"{tag}rstd", [128, TOK], F32, lstk)
    var_b = p.sb(f"{tag}var", [128, TOK], F32, lstk)
    sfp = [p.sb(f"{tag}sfp{i}", [128, TOK], F32, lstk) for i in range(2)]
    xout = [p.sb(f"{tag}xo{i}", [128, TOK], F32, lstk) for i in range(2)]
    pstk = contextlib.ExitStack()
    psS = p.ps(f"{tag}psS", [128, 2048], F32, stk=pstk)
    psQ = p.ps(f"{tag}psQ", [128, 2048], F32, stk=pstk)

    with nc.Block() as b:
        @b.tensor
        def _(te):
            p.waitall(te)
            for c in range(DCH):
                for n in range(NT):
                    te.matmul(psS[:, n * 512:(n + 1) * 512], onescol[:],
                              s_bf[:, c, n * 512:(n + 1) * 512],
                              start=(c == 0), stop=(c == DCH - 1))
                    mm = te.matmul(psQ[:, n * 512:(n + 1) * 512], onescol[:],
                                   sq_bf[:, c, n * 512:(n + 1) * 512],
                                   start=(c == 0), stop=(c == DCH - 1))
            p.inc(mm, f"{tag}st", 1)

        @b.vector
        def _(v):
            v.wait_ge(p.sem(f"{tag}st"), p.cnt[f"{tag}st"])
            v.tensor_scalar_mul(mean_b[:], psS[:], 1.0 / DM)
            v.tensor_scalar_mul(var_b[:], psQ[:], 1.0 / DM)
            v.tensor_tensor(rstd_b[:], mean_b[:], mean_b[:], ALU.mult)
            v.tensor_tensor(var_b[:], var_b[:], rstd_b[:], ALU.subtract)
            p.inc(v.tensor_scalar_add(var_b[:], var_b[:], float(eps)),
                  f"{tag}vv", 1)

        @b.scalar
        def _(sc):
            sc.wait_ge(p.sem(f"{tag}vv"), p.cnt[f"{tag}vv"])
            p.inc(sc.activation(var_b[:], var_b[:], AF.Ln), f"{tag}l1", 1)

        @b.vector
        def _(v):
            v.wait_ge(p.sem(f"{tag}l1"), p.cnt[f"{tag}l1"])
            p.inc(v.tensor_scalar_mul(var_b[:], var_b[:], -0.5), f"{tag}l2", 1)

        @b.scalar
        def _(sc):
            sc.wait_ge(p.sem(f"{tag}l2"), p.cnt[f"{tag}l2"])
            p.inc(sc.activation(rstd_b[:], var_b[:], AF.Exp), f"{tag}rs", 1)
    pstk.close()

    sdma = p.snap("ld")
    s_ap = p.snap(f"{tag}ap")
    s_so = p.snap(f"{tag}so")
    with nc.Block() as b:
        @b.sync
        def _(sy):
            for c in range(DCH):
                if c >= 2:
                    wge(sy, s_ap, c - 1)
                p.dma(sy, sfp[c % 2][:], s_hbm[c * 128:(c + 1) * 128, :])

        @b.vector
        def _(v):
            v.wait_ge(p.sem(f"{tag}rs"), p.cnt[f"{tag}rs"])
            for c in range(DCH):
                wge(v, sdma, (c + 1) * 16)
                if c >= 2:
                    wge(v, s_so, (c - 1) * 16)
                v.tensor_tensor(xout[c % 2][:], sfp[c % 2][:], mean_b[:],
                                ALU.subtract)
                last = v.tensor_tensor(xout[c % 2][:], xout[c % 2][:],
                                       rstd_b[:], ALU.mult)
                if gb is not None:
                    last = v.tensor_scalar(xout[c % 2][:], xout[c % 2][:],
                                           gb[0][:, c:c + 1], gb[1][:, c:c + 1],
                                           ALU.mult, ALU.add)
                if out_bf is not None:
                    last = v.tensor_copy(out_bf[:, c, :], xout[c % 2][:])
                p.inc(last, f"{tag}ap", 1)

        @b.scalar
        def _(sc):
            for c in range(DCH):
                wge(sc, s_ap, c + 1)
                p.dma(sc, out_hbm[c * 128:(c + 1) * 128, :], xout[c % 2][:],
                      f"{tag}so")

        @b.gpsimd
        def _(g):
            p.waitall(g, f"{tag}so")
    lstk.close()


def spill(p, name, t, chunks=None):
    nc = p.nc
    if chunks is None:
        sh = [t.shape[0], int(np.prod(t.shape[1:]))]
        ap = p.dram_out(name, sh, t.dtype)
        with nc.Block() as b:
            @b.sync
            def _(sy):
                p.dma(sy, ap, t.rearrange("p a b -> p (a b)") if len(t.shape) == 3 else t[:], f"sp{name}")
                p.waitall(sy, f"sp{name}")
    return


def build_l0a(stage=99):
    p = P("l0a")
    nc = p.nc
    xin = p.dram_in("xin", [DM, TOK])
    wqd = p.dram_in("wqd", [DM, DM])
    wkd = p.dram_in("wkd", [DM, DM])
    wvd = p.dram_in("wvd", [DM, DM])
    uq = p.dram_in("uq", [128, 128])
    uk = p.dram_in("uk", [128, 128])
    upetq = p.dram_in("upetq", [32, 64])
    upetk = p.dram_in("upetk", [32, 64])
    wv2 = p.dram_in("wv2", [128, 128])
    wc = p.dram_in("wc", [DM, DM])
    c_id = p.dram_in("c_id", [128, 128])
    c_ones = p.dram_in("c_ones", [128, 128])
    c_onesr = p.dram_in("c_onesr", [1, 128])
    c_cd = p.dram_in("c_cd", [128, 128])
    c_cdt = p.dram_in("c_cdt", [128, 128])
    c_rc = p.dram_in("c_rc", [128, TOK])
    c_rs = p.dram_in("c_rs", [128, TOK])
    c_rr = p.dram_in("c_rr", [32, 32])
    xout = p.dram_out("xout", [DM, TOK])
    s_hbm = p.dram_tmp("s_hbm", [DM, TOK])

    p.idsb = p.sb("idsb", [128, 128], BF16)
    p.onesr = p.sb("onesr", [1, 128], BF16)
    p.onescol = p.sb("onescol", [128, 128], BF16)
    cdm = p.sb("cdm", [128, 128], F32)
    cdtm = p.sb("cdtm", [128, 128], F32)

    s1b = contextlib.ExitStack()
    qkb = p.sb("qkb", [128, 8, TOK], BF16, s1b)
    kkb = p.sb("kkb", [128, 8, TOK], BF16, s1b)
    vtm = p.sb("vtm", [128, NQT, DM], BF16, s1b)

    s1 = contextlib.ExitStack()
    ropec = p.sb("ropec", [128, TOK], F32, s1)
    ropes = p.sb("ropes", [128, TOK], F32, s1)
    rrsb = p.sb("rrsb", [32, 32], BF16, s1)
    uqsb = p.sb("uqsb", [128, 128], BF16, s1)
    uksb = p.sb("uksb", [128, 128], BF16, s1)
    upetq_sb = p.sb("upetq_sb", [32, 64], BF16, s1)
    upetk_sb = p.sb("upetk_sb", [32, 64], BF16, s1)
    urq = p.sb("urq", [128, 128], BF16, s1)
    urk = p.sb("urk", [128, 128], BF16, s1)
    wv2sb = p.sb("wv2sb", [128, 128], BF16, s1)
    xbf = p.sb("xbf", [128, DCH, TOK], BF16, s1)
    wdb = p.sb("wdb", [128, DCH, DM], BF16, s1)
    latb = p.sb("latb", [128, DCH, TOK], BF16, s1)
    tmp1 = p.sb("tmp1", [128, TOK], BF16, s1)
    tmp2 = p.sb("tmp2", [128, TOK], BF16, s1)

    with nc.Block() as b:
        @b.gpsimd
        def _(g):
            cast_load(p, g, xbf, xin, DCH)
            p.dma(g, p.idsb[:], c_id)
            p.dma(g, p.onesr[:], c_onesr)
            p.dma(g, p.onescol[:], c_ones)
            p.dma(g, rrsb[:], c_rr)
            p.dma(g, uqsb[:], uq)
            p.dma(g, uksb[:], uk)
            p.dma(g, upetq_sb[:], upetq)
            p.dma(g, upetk_sb[:], upetk)
            p.dma(g, wv2sb[:], wv2)
            cast_load(p, g, wdb, wqd, DCH)

        @b.sync
        def _(sy):
            p.dma(sy, cdm[:], c_cd)
            p.dma(sy, cdtm[:], c_cdt)
            p.dma(sy, ropec[:], c_rc)
            p.dma(sy, ropes[:], c_rs)

    pstk = contextlib.ExitStack()
    psu = p.ps("psu", [64, 32], F32, stk=pstk)
    psu2 = p.ps("psu2", [64, 32], F32, stk=pstk)
    with nc.Block() as b:
        @b.gpsimd
        def _(g):
            g.memset(urq[:], 0.0)
            p.inc(g.memset(urk[:], 0.0), "urz", 1)

        @b.tensor
        def _(te):
            p.waitall(te)
            te.matmul(psu[:], upetq_sb[:], rrsb[:], start=True, stop=True)
            p.inc(te.matmul(psu2[:], upetk_sb[:], rrsb[:], start=True,
                            stop=True), "urm", 1)

        @b.vector
        def _(v):
            v.wait_ge(p.sem("urm"), p.cnt["urm"])
            v.wait_ge(p.sem("urz"), p.cnt["urz"])
            v.tensor_copy(urq[0:64, 32:64], psu[:])
            v.tensor_copy(urq[64:128, 96:128], psu[:])
            v.tensor_copy(urk[0:64, 32:64], psu2[:])
            v.tensor_copy(urk[64:128, 96:128], psu2[:])
    pstk.close()

    if stage <= 1:
        spill(p, "dbg", xbf)
        s1.close(); s1b.close(); p.stk.close()
        return p

    def lat_evac(v, m, n, ps):
        return v.tensor_copy(latb[:, m, n * 512:(n + 1) * 512], ps[:])

    emit_linear_fm(p, "latq", wdb, xbf, DCH, DCH, lat_evac)
    if stage <= 2:
        spill(p, "dbg", latb)
        s1.close(); s1b.close(); p.stk.close()
        return p
    _emit_qk_up(p, "qup", latb, uqsb, urq, ropec, ropes, qkb, tmp1, tmp2)
    if stage <= 3:
        spill(p, "dbg", qkb)
        s1.close(); s1b.close(); p.stk.close()
        return p
    with nc.Block() as b:
        @b.gpsimd
        def _(g):
            cast_load(p, g, wdb, wkd, DCH)
    emit_linear_fm(p, "latk", wdb, xbf, DCH, DCH, lat_evac)
    _emit_qk_up(p, "kup", latb, uksb, urk, ropec, ropes, kkb, tmp1, tmp2)
    with nc.Block() as b:
        @b.gpsimd
        def _(g):
            cast_load(p, g, wdb, wvd, DCH)
    emit_linear_fm(p, "latv", wdb, xbf, DCH, DCH, lat_evac)
    _emit_vtm_l0(p, "vtm", latb, wv2sb, vtm)
    if stage <= 4:
        spill(p, "dbg", vtm)
        s1.close(); s1b.close(); p.stk.close()
        return p

    s1.close()

    s1c = contextlib.ExitStack()
    otm = p.sb("otm", [128, NQT, DM], BF16, s1c, side="right")
    p.att_side = "right"
    abks, btls = make_causal_cbs(cdm[:], cdtm[:])
    emit_attention(p, "att", 16, qkb, kkb, vtm, otm, abks, btls,
                   cdm, cdtm, stage=stage)
    if stage <= 7:
        spill(p, "dbg", otm)
        if hasattr(p, "_att_astk"):
            p._att_astk.close()
        s1c.close(); s1b.close(); p.stk.close()
        return p
    s1b.close()

    s3 = contextlib.ExitStack()
    obf = p.sb("obf", [128, DCH, TOK], BF16, s3)
    _emit_tm2fm(p, "o2f", otm, obf, DCH)
    if stage <= 8:
        spill(p, "dbg", obf)
        s1c.close(); s3.close(); p.stk.close()
        return p
    s1c.close()

    wcb = p.sb("wcb", [128, DCH, DM], BF16, s3)
    s_bf = p.sb("s_bf", [128, DCH, TOK], BF16, s3)
    sq_bf = p.sb("sq_bf", [128, DCH, TOK], BF16, s3)
    sfp = [p.sb(f"sfp{i}", [128, TOK], F32, s3) for i in range(2)]
    xres = [p.sb(f"xres{i}", [128, TOK], F32, s3) for i in range(2)]
    with nc.Block() as b:
        @b.gpsimd
        def _(g):
            cast_load(p, g, wcb, wc, DCH)
    _emit_proj_res(p, "proj", wcb, obf, xin, s_hbm, s_bf, sq_bf, sfp, xres)

    emit_postln(p, "ln", s_hbm, s_bf, sq_bf, xout)
    s3.close()
    p.stk.close()
    return p


def _emit_qk_up(p, tag, latb, usb, ursb, ropec, ropes, outb, tmp1, tmp2):
    nc = p.nc
    pstk = contextlib.ExitStack()
    psu = p.ps(f"{tag}psu", [128, 2048], F32, stk=pstk)
    psr = p.ps(f"{tag}psr", [128, 2048], F32, stk=pstk)
    s_pe, s_dv = p.snap(f"{tag}pe"), p.snap(f"{tag}dv")

    with nc.Block() as b:
        @b.tensor
        def _(te):
            p.waitall(te)
            for c in range(DCH):
                if c >= 1:
                    wge(te, s_dv, c)
                for n in range(NT):
                    te.matmul(psu[:, n * 512:(n + 1) * 512], usb[:],
                              latb[:, c, n * 512:(n + 1) * 512],
                              start=True, stop=True)
                for n in range(NT):
                    mm = te.matmul(psr[:, n * 512:(n + 1) * 512], ursb[:],
                                   latb[:, c, n * 512:(n + 1) * 512],
                                   start=True, stop=True)
                p.inc(mm, f"{tag}pe", 1)

        @b.vector
        def _(v):
            for c in range(DCH):
                wge(v, s_pe, c + 1)
                v.tensor_tensor(tmp1[:], psu[:], ropec[:], ALU.mult)
                v.tensor_tensor(tmp2[:], psr[:], ropes[:], ALU.mult)
                last = v.tensor_tensor(outb[:, c, :], tmp1[:], tmp2[:],
                                       ALU.add)
                p.inc(last, f"{tag}dv", 1)
    pstk.close()


def _emit_vtm_l0(p, tag, latb, wv2sb, vtm):
    nc = p.nc
    pstk = contextlib.ExitStack()
    psv = [p.ps(f"{tag}ps{i}", [128, 128], F32, stk=pstk) for i in range(2)]
    s_pe, s_dv = p.snap(f"{tag}pe"), p.snap(f"{tag}dv")
    sched = [(c, t) for c in range(DCH) for t in range(NQT)]
    with nc.Block() as b:
        @b.tensor
        def _(te):
            p.waitall(te)
            for i, (c, t) in enumerate(sched):
                if i >= 2:
                    wge(te, s_dv, i - 1)
                mm = te.matmul(psv[i % 2][:],
                               latb[:, c, t * 128:(t + 1) * 128],
                               wv2sb[:], start=True, stop=True)
                p.inc(mm, f"{tag}pe", 1)

        @b.vector
        def _(v):
            for i, (c, t) in enumerate(sched):
                wge(v, s_pe, i + 1)
                last = v.tensor_copy(vtm[:, t, c * 128:(c + 1) * 128],
                                     psv[i % 2][:])
                p.inc(last, f"{tag}dv", 1)
    pstk.close()


def _emit_tm2fm(p, tag, tmb, fmb, chunks):
    nc = p.nc
    pstk = contextlib.ExitStack()
    pst = [p.ps(f"{tag}ps{i}", [128, 128], BF16, stk=pstk) for i in range(2)]
    s_pe, s_dv = p.snap(f"{tag}pe"), p.snap(f"{tag}dv")
    sched = [(c, t) for c in range(chunks) for t in range(NQT)]
    with nc.Block() as b:
        @b.tensor
        def _(te):
            p.waitall(te)
            for i, (c, t) in enumerate(sched):
                if i >= 2:
                    wge(te, s_dv, i - 1)
                mm = te.transpose(pst[i % 2][:],
                                  tmb[:, t, c * 128:(c + 1) * 128], p.idsb[:])
                p.inc(mm, f"{tag}pe", 1)

        @b.vector
        def _(v):
            for i, (c, t) in enumerate(sched):
                wge(v, s_pe, i + 1)
                last = v.tensor_copy(fmb[:, c, t * 128:(t + 1) * 128],
                                     pst[i % 2][:])
                p.inc(last, f"{tag}dv", 1)
    pstk.close()


def _emit_proj_res(p, tag, w_sb, rhs_sb, xres_hbm, s_hbm, s_bf, sq_bf,
                   sfp, xres, bias_sb=None):
    nc = p.nc
    pstk = contextlib.ExitStack()
    psl = [p.ps(f"{tag}ps{i}", [128, 512], F32, stk=pstk) for i in range(4)]
    s_pe, s_dv, s_xr, s_sq = (p.snap(f"{tag}pe"), p.snap(f"{tag}dv"),
                              p.snap(f"{tag}xr"), p.snap(f"{tag}sq"))
    s_st, s_vs = p.snap(f"{tag}st"), p.snap(f"{tag}vs")
    with nc.Block() as b:
        @b.sync
        def _(sy):
            for m in range(DCH):
                if m >= 2:
                    wge(sy, s_dv, (m - 1) * NT)
                p.dma(sy, xres[m % 2][:], xres_hbm[m * 128:(m + 1) * 128, :],
                      f"{tag}xr")

        @b.tensor
        def _(te):
            p.waitall(te)
            t = 0
            for m in range(DCH):
                for n in range(NT):
                    if t >= 4:
                        wge(te, s_dv, t - 3)
                    ps = psl[t % 4]
                    for k in range(DCH):
                        mm = te.matmul(ps[:],
                                       w_sb[:, k, m * 128:(m + 1) * 128],
                                       rhs_sb[:, k, n * 512:(n + 1) * 512],
                                       start=(k == 0), stop=(k == DCH - 1))
                    p.inc(mm, f"{tag}pe", 1)
                    t += 1

        @b.vector
        def _(v):
            t = 0
            for m in range(DCH):
                wge(v, s_xr, 16 * (m + 1))
                if m >= 2:
                    wge(v, s_st, 16 * (m - 1))
                    wge(v, s_sq, m - 1)
                for n in range(NT):
                    wge(v, s_pe, t + 1)
                    if bias_sb is not None:
                        v.scalar_tensor_tensor(
                            sfp[m % 2][:, n * 512:(n + 1) * 512], psl[t % 4][:],
                            bias_sb[:, m:m + 1],
                            xres[m % 2][:, n * 512:(n + 1) * 512],
                            ALU.add, ALU.add)
                    else:
                        v.tensor_tensor(sfp[m % 2][:, n * 512:(n + 1) * 512],
                                        psl[t % 4][:],
                                        xres[m % 2][:, n * 512:(n + 1) * 512],
                                        ALU.add)
                    p.inc(v.engine_nop(), f"{tag}dv", 1)
                    t += 1
                last = v.tensor_copy(s_bf[:, m, :], sfp[m % 2][:])
                p.inc(last, f"{tag}vs", 1)

        @b.scalar
        def _(sc):
            for m in range(DCH):
                wge(sc, s_vs, m + 1)
                last = sc.activation(sq_bf[:, m, :], sfp[m % 2][:], AF.Square)
                p.inc(last, f"{tag}sq", 1)
                p.dma(sc, s_hbm[m * 128:(m + 1) * 128, :], sfp[m % 2][:],
                      f"{tag}st")

        @b.gpsimd
        def _(g):
            p.waitall(g, f"{tag}st")
    pstk.close()


def prep_l0a(inputs, xfm_percore):
    base = {
        "wqd": inputs["L0_attn_Wq_down"],
        "wkd": inputs["L0_attn_Wk_down"],
        "wvd": inputs["L0_attn_Wv_down"],
        "uq": blockdiag2(np.concatenate(
            [inputs["L0_attn_Wq_up_c"], inputs["L0_attn_Wq_up_e"]], axis=1)),
        "uk": blockdiag2(np.concatenate(
            [inputs["L0_attn_Wk_up_c"], inputs["L0_attn_Wk_up_e"]], axis=1)),
        "upetq": np.ascontiguousarray(inputs["L0_attn_Wq_up_e"].T),
        "upetk": np.ascontiguousarray(inputs["L0_attn_Wk_up_e"].T),
        "wv2": blockdiag2(inputs["L0_attn_Wv_up"]),
        "wc": inputs["L0_attn_Wc"],
        "c_id": HC["ident"], "c_ones": HC["onescol"], "c_onesr": HC["onesrow"],
        "c_cd": HC["cdiag"], "c_cdt": HC["cdiagT"],
        "c_rc": HC["ropecos"], "c_rs": HC["ropesin"], "c_rr": HC["ropeR"],
    }
    maps = []
    for b in range(NCORE):
        m = dict(base)
        m["xin"] = xfm_percore[b]
        maps.append(m)
    return maps


def run_launch(build_fn, maps, outs):
    pr = build_fn()
    res = run_bass_kernel_spmd(pr.nc, maps, core_ids=list(range(NCORE)))
    return [{o: r[o] for o in outs} for r in res.results]


def kernel(**inputs):
    inputs = {k: np.asarray(v) for k, v in inputs.items()}
    x = inputs["L0_attn_x"]
    xfm = [np.ascontiguousarray(x[b].T) for b in range(NCORE)]

    res = run_launch(build_l0a, prep_l0a(inputs, xfm), ["xout"])
    xfm = [r["xout"] for r in res]
    res = run_launch(build_l0f, prep_l0f(inputs, xfm), ["xout"])
    xfm = [r["xout"] for r in res]
    res = run_launch(build_l1a, prep_l1a(inputs, xfm), ["xout"])
    xfm = [r["xout"] for r in res]
    res = run_launch(build_l1f, prep_l1f(inputs, xfm), ["xout"])
    xfm = [r["xout"] for r in res]
    res = run_launch(build_l2a, prep_l2a(inputs, xfm), ["xout"])
    xfm = [r["xout"] for r in res]

    x5f = np.stack([f.T for f in xfm], axis=0)
    xps = []
    for j in range(NCORE):
        slab = x5f[:, 256 * j:256 * (j + 1), :]
        xps.append(np.ascontiguousarray(
            slab.transpose(2, 1, 0).reshape(DM, TOK)))

    res = run_launch(build_l2fa, prep_l2fa(inputs, xps), ["xout", "lgout"])
    yps = [r["xout"] for r in res]
    lgfull = np.zeros((16384, 8), np.float32)
    for j in range(NCORE):
        lr = res[j]["lgout"].reshape(256, 8, 8)
        for b in range(NCORE):
            lgfull[b * 2048 + 256 * j:b * 2048 + 256 * j + 256] = lr[:, b, :]

    res = run_launch(build_l2fb, prep_l2fb(inputs, yps, lgfull),
                     ["selr", "wcd", "bcout"])
    maps_c = prep_l2fc(inputs, yps, res[0]["selr"], res[0]["bcout"],
                       res[0]["wcd"])
    for j in range(NCORE):
        maps_c[j]["x5in"] = xps[j]
    res = run_launch(build_l2fc, maps_c, ["xout"])

    x6 = np.concatenate(
        [res[j]["xout"].reshape(DM, 256, 8).transpose(2, 1, 0)
         for j in range(NCORE)], axis=1)
    xfm = [np.ascontiguousarray(x6[b].T) for b in range(NCORE)]

    res = run_launch(build_l3a, prep_l3a(inputs, xfm), ["xout"])
    xfm = [r["xout"] for r in res]
    res = run_launch(build_l3f, prep_l3f(inputs, xfm), ["xout"])
    xfm = [r["xout"] for r in res]

    return np.ascontiguousarray(
        np.stack([r.T for r in xfm], axis=0)).astype(np.float32)


def build_l0f(stage=99):
    p = P("l0f")
    nc = p.nc
    xin = p.dram_in("xin", [DM, TOK])
    rw = p.dram_in("rw", [DM, 8])
    rs = p.dram_in("rs", [128, DCH])
    pes = p.dram_in("pes", [128, 8])
    g0t = p.dram_in("g0t", [8, DM, 2048])
    g1t = p.dram_in("g1t", [8, DM, 2048])
    lw = p.dram_in("lw", [8, 2048, DM])
    c_id = p.dram_in("c_id", [128, 128])
    c_idf = p.dram_in("c_idf", [128, 128])
    c_ones = p.dram_in("c_ones", [128, 128])
    c_ones1 = p.dram_in("c_ones1", [128, 1])
    xout = p.dram_out("xout", [DM, TOK])
    s_hbm = p.dram_tmp("s_hbm", [DM, TOK])

    p.idsb = p.sb("idsb", [128, 128], BF16)
    p.onescol = p.sb("onescol", [128, 128], BF16)
    idf32 = p.sb("idf32", [128, 128], F32)
    ones1 = p.sb("ones1", [128, 1], BF16)
    rwb = p.sb("rwb", [128, DCH, 8], F32)
    rssb = p.sb("rssb", [128, DCH], F32)
    rsc = p.sb("rsc", [128, 8], F32)
    prb = p.sb("prb", [128, 8], F32)
    msk = p.sb("msk", [128, 8], F32)
    lacc = p.sb("lacc", [128, NQT, 8], F32)
    pesb = p.sb("pesb", [128, 8], F32)
    combT = p.sb("combT", [128, NQT, 8], F32)
    t16 = p.sb("t16", [128, NQT], F32)
    logsb = p.sb("logsb", [128, NQT, 8], F32)
    accstk = contextlib.ExitStack()
    acc = p.sb("acc", [128, NQT, DM], F32, accstk, side="right")
    rstk = contextlib.ExitStack()
    xf32 = [p.sb(f"xf32{i}", [128, TOK], F32, rstk, side="right") for i in range(2)]
    estk = contextlib.ExitStack()
    xbf = p.sb("xbf", [128, DCH, TOK], BF16, estk)

    with nc.Block() as b:
        @b.gpsimd
        def _(g):
            cast_load(p, g, xbf, xin, DCH)
            g.memset(lacc.rearrange("p a b -> p (a b)"), 0.0)
            p.inc(g.engine_nop(), "laz", 1)
            p.dma(g, p.idsb[:], c_id)
            p.dma(g, p.onescol[:], c_ones)
            p.dma(g, ones1[:], c_ones1)
            g.memset(acc.rearrange("p a b -> p (a b)"), 0.0)
            p.inc(g.engine_nop(), "accz", 1)

        @b.sync
        def _(sy):
            p.dma(sy, rssb[:], rs)
            p.dma(sy, pesb[:], pes)
            p.dma(sy, idf32[:], c_idf)
            for c in range(DCH):
                p.dma(sy, rwb[:, c, :], rw[c * 128:(c + 1) * 128, :])

    with nc.Block() as b:
        @b.vector
        def _(v):
            p.waitall(v)
            v.tensor_tensor(rwb[:], rwb[:],
                            rssb[:, :].unsqueeze(2).broadcast_to([128, DCH, 8]),
                            ALU.mult)

    sqb = p.sb("sqb", [128, DCH, TOK], BF16, rstk, side="right")
    with nc.Block() as b:
        @b.scalar
        def _(sc):
            p.waitall(sc)
            p.inc(sc.activation(sqb.rearrange("p a b -> p (a b)"),
                                xbf.rearrange("p a b -> p (a b)"), AF.Square),
                  "l0fsq", 1)

    pstk = contextlib.ExitStack()
    psq = [p.ps(f"l0fpsq{i}", [128, 1], F32, stk=pstk) for i in range(2)]
    s_pe, s_dv = p.snap("l0fqpe"), p.snap("l0fqdv")
    with nc.Block() as b:
        @b.tensor
        def _(te):
            te.wait_ge(p.sem("l0fsq"), p.cnt["l0fsq"])
            for t in range(NQT):
                if t >= 2:
                    wge(te, s_dv, t - 1)
                for c in range(DCH):
                    mm = te.matmul(psq[t % 2][:],
                                   sqb[:, c, t * 128:(t + 1) * 128],
                                   ones1[:], start=(c == 0), stop=(c == DCH - 1))
                p.inc(mm, "l0fqpe", 1)

        @b.vector
        def _(v):
            for t in range(NQT):
                wge(v, s_pe, t + 1)
                last = v.tensor_copy(t16[:, t:t + 1], psq[t % 2][:])
                p.inc(last, "l0fqdv", 1)
    pstk.close()

    with nc.Block() as b:
        @b.vector
        def _(v):
            p.chain(v, v.tensor_scalar_mul(t16[:], t16[:], 1.0 / DM))
            p.inc(v.tensor_scalar_add(t16[:], t16[:], 1e-6), "l0ft1", 1)

        @b.scalar
        def _(sc):
            sc.wait_ge(p.sem("l0ft1"), p.cnt["l0ft1"])
            p.inc(sc.activation(t16[:], t16[:], AF.Ln), "l0ft2", 1)

        @b.vector
        def _(v):
            v.wait_ge(p.sem("l0ft2"), p.cnt["l0ft2"])
            p.inc(v.tensor_scalar_mul(t16[:], t16[:], -0.5), "l0ft3", 1)

        @b.scalar
        def _(sc):
            sc.wait_ge(p.sem("l0ft3"), p.cnt["l0ft3"])
            p.inc(sc.activation(t16[:], t16[:], AF.Exp), "l0ft4", 1)

        @b.vector
        def _(v):
            v.wait_ge(p.sem("l0ft4"), p.cnt["l0ft4"])
            v.tensor_scalar_mul(t16[:], t16[:], 1.0 / 32.0)

    pstk = contextlib.ExitStack()
    psl = [p.ps(f"l0flg{i}", [128, 8], F32, stk=pstk) for i in range(2)]
    s_pe2, s_dv2 = p.snap("l0flpe"), p.snap("l0fldv")
    s_xr2 = p.snap("l0fxr")
    sched_l = [(c, t) for c in range(DCH) for t in range(NQT)]
    with nc.Block() as b:
        @b.sync
        def _(sy):
            for c in range(DCH):
                if c >= 2:
                    wge(sy, s_dv2, (c - 1) * NQT)
                p.dma(sy, xf32[c % 2][:], xin[c * 128:(c + 1) * 128, :],
                      "l0fxr")

        @b.tensor
        def _(te):
            for i, (c, t) in enumerate(sched_l):
                wge(te, s_xr2, (c + 1) * 16)
                if i >= 2:
                    wge(te, s_dv2, i - 1)
                mm = te.matmul(psl[i % 2][:],
                               xf32[c % 2][:, t * 128:(t + 1) * 128],
                               rwb[:, c, :], start=True, stop=True)
                p.inc(mm, "l0flpe", 1)

        @b.vector
        def _(v):
            v.wait_ge(p.sem("laz"), p.cnt["laz"])
            for i, (c, t) in enumerate(sched_l):
                wge(v, s_pe2, i + 1)
                last = v.tensor_tensor(lacc[:, t, :], lacc[:, t, :],
                                       psl[i % 2][:], ALU.add)
                p.inc(last, "l0fldv", 1)
    pstk.close()

    with nc.Block() as b:
        @b.vector
        def _(v):
            for t in range(NQT):
                v.tensor_scalar(logsb[:, t, :], lacc[:, t, :],
                                t16[:, t:t + 1], None, ALU.mult)
    if stage <= 1:
        spill(p, "dbg", logsb)
        spill(p, "dbgt16", t16)
        spill(p, "dbgrwb", rwb)
        p.stk.close()
        return p

    s_rt, s_re = p.snap("l0frt"), p.snap("l0fre")
    with nc.Block() as b:
        @b.vector
        def _(v):
            for t in range(NQT):
                lg = logsb[:, t, :]
                last = v.reduce_max(rsc[:, 0:1], lg, axis=AX.X, negate=True)
                p.inc(last, "l0frt", 1)
                wge(v, s_re, t + 1)
                p.chain(v, v.reciprocal(rsc[:, 2:3], rsc[:, 1:2]))
                p.chain(v, v.tensor_scalar(prb[:], prb[:], rsc[:, 2:3], None,
                                           ALU.mult))
                p.chain(v, v.tensor_scalar_mul(rsc[:, 3:4], rsc[:, 0:1], -1.0))
                p.chain(v, v.tensor_tensor(msk[:], lg,
                                           rsc[:, 3:4].broadcast_to([128, 8]),
                                           ALU.is_ge))
                p.chain(v, v.scalar_tensor_tensor(combT[:, t, :], msk[:], NEG,
                                                  lg, ALU.mult, ALU.add))
                p.chain(v, v.reduce_max(rsc[:, 4:5], combT[:, t, :], axis=AX.X))
                p.chain(v, v.tensor_tensor(combT[:, t, :], combT[:, t, :],
                                           rsc[:, 4:5].broadcast_to([128, 8]),
                                           ALU.is_ge))
                p.chain(v, v.tensor_tensor(msk[:], msk[:], combT[:, t, :],
                                           ALU.add))
                p.chain(v, v.tensor_tensor(prb[:], prb[:], msk[:], ALU.mult))
                p.chain(v, v.reduce_sum(rsc[:, 6:7], prb[:], axis=AX.X))
                p.chain(v, v.reciprocal(rsc[:, 7:8], rsc[:, 6:7]))
                p.chain(v, v.tensor_scalar(prb[:], prb[:], rsc[:, 7:8], None,
                                           ALU.mult))
                v.tensor_tensor(combT[:, t, :], prb[:], pesb[:], ALU.mult)

        @b.scalar
        def _(sc):
            for t in range(NQT):
                wge(sc, s_rt, t + 1)
                ac = sc.activation(prb[:], logsb[:, t, :], AF.Exp,
                                   bias=rsc[:, 0:1], accum_out=rsc[:, 1:2])
                p.inc(ac, "l0fre", 1)
    if stage <= 2:
        spill(p, "dbg", combT)
        p.stk.close()
        return p

    rstk.close()
    wg0 = p.sb("wg0", [128, DCH, 1024], BF16, estk)
    wg1 = p.sb("wg1", [128, DCH, 1024], BF16, estk)
    wlw = p.sb("wlw", [128, DCH, 1024], BF16, estk)
    actb = p.sb("actb", [128, DCH, TOK], BF16, estk)
    g0f = [p.sb(f"g0f{i}", [128, 512], F32, estk) for i in range(2)]

    for e in range(8):
        for hf in range(2):
            tagg = f"e{e}h{hf}"
            with nc.Block() as b:
                @b.gpsimd
                def _(g):
                    for k in range(DCH):
                        p.dma(g, wg0[:, k, :],
                              g0t[e, k * 128:(k + 1) * 128,
                                  hf * 1024:(hf + 1) * 1024])
                        p.dma(g, wg1[:, k, :],
                              g1t[e, k * 128:(k + 1) * 128,
                                  hf * 1024:(hf + 1) * 1024])
                    for k in range(DCH):
                        p.dma(g, wlw[:, k, :],
                              lw[e, hf * 1024 + k * 128:hf * 1024 +
                                 (k + 1) * 128, :])

            pstk = contextlib.ExitStack()
            ps0 = [p.ps(f"{tagg}a{i}", [128, 512], F32, stk=pstk)
                   for i in range(2)]
            ps1 = [p.ps(f"{tagg}b{i}", [128, 512], F32, stk=pstk)
                   for i in range(2)]
            s_pe3, s_ac3, s_dv3 = (p.snap("egpe"), p.snap("egac"),
                                   p.snap("egdv"))
            sched = [(m, n) for m in range(8) for n in range(NT)]
            with nc.Block() as b:
                @b.tensor
                def _(te):
                    p.waitall(te)
                    for t, (m, n) in enumerate(sched):
                        if t >= 2:
                            wge(te, s_dv3, t - 1)
                        for k in range(DCH):
                            te.matmul(ps0[t % 2][:],
                                      wg0[:, k, m * 128:(m + 1) * 128],
                                      xbf[:, k, n * 512:(n + 1) * 512],
                                      start=(k == 0), stop=(k == DCH - 1))
                        for k in range(DCH):
                            mm = te.matmul(ps1[t % 2][:],
                                           wg1[:, k, m * 128:(m + 1) * 128],
                                           xbf[:, k, n * 512:(n + 1) * 512],
                                           start=(k == 0), stop=(k == DCH - 1))
                        p.inc(mm, "egpe", 1)

                @b.scalar
                def _(sc):
                    for t, (m, n) in enumerate(sched):
                        wge(sc, s_pe3, t + 1)
                        if t >= 2:
                            wge(sc, s_dv3, t - 1)
                        ac = sc.activation(g0f[t % 2][:], ps0[t % 2][:],
                                           AF.Gelu_apprx_tanh)
                        p.inc(ac, "egac", 1)

                @b.vector
                def _(v):
                    for t, (m, n) in enumerate(sched):
                        wge(v, s_ac3, t + 1)
                        last = v.tensor_tensor(
                            actb[:, m, n * 512:(n + 1) * 512],
                            g0f[t % 2][:], ps1[t % 2][:], ALU.mult)
                        p.inc(last, "egdv", 1)
            pstk.close()

            pstk = contextlib.ExitStack()
            psd = [p.ps(f"{tagg}d{i}", [128, 512], F32, stk=pstk)
                   for i in range(4)]
            s_pe4, s_dv4 = p.snap("edpe"), p.snap("eddv")
            dsched = [(m, n) for m in range(NQT) for n in range(2)]
            with nc.Block() as b:
                @b.tensor
                def _(te):
                    if e == 0 and hf == 0:
                        te.wait_ge(p.sem("accz"), p.cnt["accz"])
                    for t, (m, n) in enumerate(dsched):
                        if t >= 4:
                            wge(te, s_dv4, t - 3)
                        for k in range(DCH):
                            mm = te.matmul(psd[t % 4][:],
                                           actb[:, k, m * 128:(m + 1) * 128],
                                           wlw[:, k, n * 512:(n + 1) * 512],
                                           start=(k == 0), stop=(k == DCH - 1))
                        p.inc(mm, "edpe", 1)

                @b.vector
                def _(v):
                    for t, (m, n) in enumerate(dsched):
                        wge(v, s_pe4, t + 1)
                        last = v.scalar_tensor_tensor(
                            acc[:, m, n * 512:(n + 1) * 512], psd[t % 4][:],
                            combT[:, m, e:e + 1],
                            acc[:, m, n * 512:(n + 1) * 512],
                            ALU.mult, ALU.add)
                        p.inc(last, "eddv", 1)
            pstk.close()

    if stage <= 3:
        spill(p, "dbg", acc)
        p.stk.close()
        return p

    estk.close()
    s3 = contextlib.ExitStack()
    s_bf = p.sb("s_bf", [128, DCH, TOK], BF16, s3)
    sq_bf = p.sb("sq_bf", [128, DCH, TOK], BF16, s3)
    sfp = [p.sb(f"sfp{i}", [128, TOK], F32, s3) for i in range(2)]
    xres = [p.sb(f"xres{i}", [128, TOK], F32, s3) for i in range(2)]
    _emit_tm2fm_res(p, "fo", acc, idf32, xin, s_hbm, s_bf, sq_bf, sfp, xres)
    accstk.close()
    emit_postln(p, "ln", s_hbm, s_bf, sq_bf, xout)
    s3.close()
    p.stk.close()
    return p


def _emit_tm2fm_res(p, tag, acc, idf32, xres_hbm, s_hbm, s_bf, sq_bf,
                    sfp, xres):
    nc = p.nc
    pstk = contextlib.ExitStack()
    pst = [p.ps(f"{tag}ps{i}", [128, 128], F32, stk=pstk) for i in range(4)]
    s_pe, s_dv, s_xr, s_sq = (p.snap(f"{tag}pe"), p.snap(f"{tag}dv"),
                              p.snap(f"{tag}xr"), p.snap(f"{tag}sq"))
    s_st, s_vs = p.snap(f"{tag}st"), p.snap(f"{tag}vs")
    with nc.Block() as b:
        @b.sync
        def _(sy):
            for m in range(DCH):
                if m >= 2:
                    wge(sy, s_dv, (m - 1) * NQT)
                p.dma(sy, xres[m % 2][:], xres_hbm[m * 128:(m + 1) * 128, :],
                      f"{tag}xr")

        @b.tensor
        def _(te):
            p.waitall(te)
            t = 0
            for m in range(DCH):
                for q in range(NQT):
                    if t >= 4:
                        wge(te, s_dv, t - 3)
                    mm = te.transpose(pst[t % 4][:],
                                      acc[:, q, m * 128:(m + 1) * 128],
                                      idf32[:])
                    p.inc(mm, f"{tag}pe", 1)
                    t += 1

        @b.vector
        def _(v):
            t = 0
            for m in range(DCH):
                wge(v, s_xr, 16 * (m + 1))
                if m >= 2:
                    wge(v, s_st, 16 * (m - 1))
                    wge(v, s_sq, m - 1)
                for q in range(NQT):
                    wge(v, s_pe, t + 1)
                    v.tensor_tensor(sfp[m % 2][:, q * 128:(q + 1) * 128],
                                    pst[t % 4][:],
                                    xres[m % 2][:, q * 128:(q + 1) * 128],
                                    ALU.add)
                    p.inc(v.engine_nop(), f"{tag}dv", 1)
                    t += 1
                last = v.tensor_copy(s_bf[:, m, :], sfp[m % 2][:])
                p.inc(last, f"{tag}vs", 1)

        @b.scalar
        def _(sc):
            for m in range(DCH):
                wge(sc, s_vs, m + 1)
                last = sc.activation(sq_bf[:, m, :], sfp[m % 2][:], AF.Square)
                p.inc(last, f"{tag}sq", 1)
                p.dma(sc, s_hbm[m * 128:(m + 1) * 128, :], sfp[m % 2][:],
                      f"{tag}st")

        @b.gpsimd
        def _(g):
            p.waitall(g, f"{tag}st")
    pstk.close()


def prep_l0f(inputs, xfm_percore):
    gw = np.asarray(inputs["L0_ffn_gating_w"])
    base = {
        "rw": inputs["L0_ffn_router_w"],
        "rs": np.ascontiguousarray(
            np.asarray(inputs["L0_ffn_router_scale"]).reshape(DCH, 128).T),
        "pes": np.ascontiguousarray(np.broadcast_to(
            np.asarray(inputs["L0_ffn_per_expert_scale"]), (128, 8))),
        "g0t": np.ascontiguousarray(gw[:, 0].transpose(0, 2, 1)),
        "g1t": np.ascontiguousarray(gw[:, 1].transpose(0, 2, 1)),
        "lw": inputs["L0_ffn_linear_w"],
        "c_id": HC["ident"], "c_idf": HC["ident"], "c_ones": HC["onescol"],
        "c_ones1": np.ones((128, 1), np.float32),
    }
    maps = []
    for b in range(NCORE):
        m = dict(base)
        m["xin"] = xfm_percore[b]
        maps.append(m)
    return maps


def _emit_vtm_gen(p, tag, xbf, w_sb, wcol0, vtm, KCH=DCH, vw=DM):
    nc = p.nc
    pstk = contextlib.ExitStack()
    psv = [p.ps(f"{tag}ps{i}", [128, 512], F32, stk=pstk) for i in range(4)]
    s_pe, s_dv = p.snap(f"{tag}pe"), p.snap(f"{tag}dv")
    nn = vw // 512
    sched = [(t, n) for t in range(NQT) for n in range(nn)]
    with nc.Block() as b:
        @b.tensor
        def _(te):
            p.waitall(te)
            for i, (t, n) in enumerate(sched):
                if i >= 4:
                    wge(te, s_dv, i - 3)
                for k in range(KCH):
                    mm = te.matmul(psv[i % 4][:],
                                   xbf[:, k, t * 128:(t + 1) * 128],
                                   w_sb[:, k, wcol0 + n * 512:
                                        wcol0 + (n + 1) * 512],
                                   start=(k == 0), stop=(k == KCH - 1))
                p.inc(mm, f"{tag}pe", 1)

        @b.vector
        def _(v):
            for i, (t, n) in enumerate(sched):
                wge(v, s_pe, i + 1)
                last = v.tensor_copy(vtm[:, t, n * 512:(n + 1) * 512],
                                     psv[i % 4][:])
                p.inc(last, f"{tag}dv", 1)
    pstk.close()


def build_l1a(stage=99):
    p = P("l1a")
    nc = p.nc
    xin = p.dram_in("xin", [DM, TOK])
    watT = p.dram_in("watT", [DM, 3 * DM])
    wpT = p.dram_in("wpT", [DM, DM])
    c_id = p.dram_in("c_id", [128, 128])
    c_ones = p.dram_in("c_ones", [128, 128])
    c_onesr = p.dram_in("c_onesr", [1, 128])
    c_cd = p.dram_in("c_cd", [128, 128])
    c_cdt = p.dram_in("c_cdt", [128, 128])
    xout = p.dram_out("xout", [DM, TOK])
    s_hbm = p.dram_tmp("s_hbm", [DM, TOK])
    if stage == 95:
        stage = 99

    p.idsb = p.sb("idsb", [128, 128], BF16)
    p.onesr = p.sb("onesr", [1, 128], BF16)
    p.onescol = p.sb("onescol", [128, 128], BF16)
    cdm = p.sb("cdm", [128, 128], F32)
    cdtm = p.sb("cdtm", [128, 128], F32)

    s1b = contextlib.ExitStack()
    qkb = p.sb("qkb", [128, 8, TOK], BF16, s1b)
    kkb = p.sb("kkb", [128, 8, TOK], BF16, s1b)
    vtm = p.sb("vtm", [128, NQT, DM], BF16, s1b)

    s1 = contextlib.ExitStack()
    xbf = p.sb("xbf", [128, DCH, TOK], BF16, s1)
    watb = p.sb("watb", [128, DCH, 3 * DM], BF16, s1)

    with nc.Block() as b:
        @b.gpsimd
        def _(g):
            cast_load(p, g, xbf, xin, DCH)
            cast_load(p, g, watb, watT, DCH)
            p.dma(g, p.idsb[:], c_id)
            p.dma(g, p.onesr[:], c_onesr)
            p.dma(g, p.onescol[:], c_ones)

        @b.sync
        def _(sy):
            p.dma(sy, cdm[:], c_cd)
            p.dma(sy, cdtm[:], c_cdt)

    def qk_evac(v, m, n, ps):
        if m < 8:
            return v.tensor_scalar(qkb[:, m, n * 512:(n + 1) * 512], ps[:],
                                   1.0 / 8.0, None, ALU.mult)
        return v.tensor_copy(kkb[:, m - 8, n * 512:(n + 1) * 512], ps[:])

    emit_linear_fm(p, "l1qk", watb, xbf, DCH, 16, qk_evac)
    _emit_vtm_gen(p, "l1v", xbf, watb, 2048, vtm)
    if stage <= 4:
        spill(p, "dbg", vtm)
        spill(p, "dbgq", qkb)
        s1.close(); s1b.close(); p.stk.close()
        return p
    s1.close()

    s1c = contextlib.ExitStack()
    otm = p.sb("otm", [128, NQT, DM], BF16, s1c, side="right")
    p.att_side = "right"
    abks, btls = make_causal_cbs(cdm[:], cdtm[:])
    emit_attention(p, "att", 16, qkb, kkb, vtm, otm, abks, btls,
                   cdm, cdtm, stage=stage)
    if stage <= 7:
        spill(p, "dbg", otm)
        if hasattr(p, "_att_astk"):
            p._att_astk.close()
        s1c.close(); s1b.close(); p.stk.close()
        return p
    s1b.close()

    s3 = contextlib.ExitStack()
    obf = p.sb("obf", [128, DCH, TOK], BF16, s3)
    _emit_tm2fm(p, "o2f", otm, obf, DCH)
    s1c.close()
    if stage <= 8:
        spill(p, "dbg8", obf)
        s3.close(); s1b.close(); p.stk.close()
        return p

    wpb = p.sb("wpb", [128, DCH, DM], BF16, s3)
    s_bf = p.sb("s_bf", [128, DCH, TOK], BF16, s3)
    sq_bf = p.sb("sq_bf", [128, DCH, TOK], BF16, s3)
    sfp = [p.sb(f"sfp{i}", [128, TOK], F32, s3) for i in range(2)]
    xres = [p.sb(f"xres{i}", [128, TOK], F32, s3) for i in range(2)]
    with nc.Block() as b:
        @b.gpsimd
        def _(g):
            cast_load(p, g, wpb, wpT, DCH)
    _emit_proj_res(p, "proj", wpb, obf, xin, s_hbm, s_bf, sq_bf, sfp, xres)
    if stage <= 9:
        spill(p, "dbg9", s_bf)
        s3.close(); s1b.close(); p.stk.close()
        return p
    s1b.close()
    emit_postln(p, "ln", s_hbm, s_bf, sq_bf, xout)
    s3.close()
    p.stk.close()
    return p


def prep_l1a(inputs, xfm_percore):
    base = {
        "watT": np.ascontiguousarray(np.asarray(inputs["L1_attn_W_attn"]).T),
        "wpT": np.ascontiguousarray(np.asarray(inputs["L1_attn_W_proj"]).T),
        "c_id": HC["ident"], "c_ones": HC["onescol"], "c_onesr": HC["onesrow"],
        "c_cd": HC["cdiag"], "c_cdt": HC["cdiagT"],
    }
    maps = []
    for b in range(NCORE):
        m = dict(base)
        m["xin"] = xfm_percore[b]
        maps.append(m)
    return maps


def build_l1f(stage=99):
    p = P("l1f")
    nc = p.nc
    xin = p.dram_in("xin", [DM, TOK])
    wrT = p.dram_in("wrT", [DM, DM])
    wiT = p.dram_in("wiT", [DM, DM])
    cmat = p.dram_in("cmat", [DM, DM])
    smat = p.dram_in("smat", [DM, DM])
    bias = p.dram_in("bias", [128, DCH])
    c_ones = p.dram_in("c_ones", [128, 128])
    xout = p.dram_out("xout", [DM, TOK])
    s_hbm = p.dram_tmp("s_hbm", [DM, TOK])

    p.onescol = p.sb("onescol", [128, 128], BF16)
    biasb = p.sb("biasb", [128, DCH], F32)
    xbf = p.sb("xbf", [128, DCH, TOK], BF16)
    mb = p.sb("mb", [128, DCH, DM], BF16)

    s1 = contextlib.ExitStack()
    wrb = p.sb("wrb", [128, DCH, DM], BF16, s1)
    wib = p.sb("wib", [128, DCH, DM], BF16, s1)
    cb = p.sb("cb", [128, DCH, DM], BF16, s1)
    sb_ = p.sb("sb_", [128, DCH, DM], BF16, s1)
    sneg = p.sb("sneg", [128, DCH, DM], BF16, s1)
    pb = p.sb("pb", [128, DCH, DM], BF16, s1)
    qb = p.sb("qb", [128, DCH, DM], BF16, s1)

    with nc.Block() as b:
        @b.gpsimd
        def _(g):
            cast_load(p, g, xbf, xin, DCH)
            cast_load(p, g, wrb, wrT, DCH)
            cast_load(p, g, wib, wiT, DCH)
            cast_load(p, g, cb, cmat, DCH)
            cast_load(p, g, sb_, smat, DCH)
            p.dma(g, p.onescol[:], c_ones)

        @b.sync
        def _(sy):
            p.dma(sy, biasb[:], bias)

    with nc.Block() as b:
        @b.vector
        def _(v):
            p.waitall(v)
            v.tensor_scalar_mul(sneg.rearrange("p a b -> p (a b)"),
                                sb_.rearrange("p a b -> p (a b)"), -1.0)

    def dual_gemm(tag, l1, r1, l2, r2, out_sb, scale=None):
        pstk = contextlib.ExitStack()
        psl = [p.ps(f"{tag}ps{i}", [128, 512], F32, stk=pstk)
               for i in range(4)]
        s_pe, s_dv = p.snap(f"{tag}pe"), p.snap(f"{tag}dv")
        sched = [(m, n) for m in range(DCH) for n in range(2)]
        with nc.Block() as b2:
            @b2.tensor
            def _(te):
                p.waitall(te)
                te.wait_ge(p.sem("l1fsn"), p.cnt.get("l1fsn", 0))
                for t, (m, n) in enumerate(sched):
                    if t >= 4:
                        wge(te, s_dv, t - 3)
                    ps = psl[t % 4]
                    for k in range(DCH):
                        te.matmul(ps[:], l1[:, k, m * 128:(m + 1) * 128],
                                  r1[:, k, n * 512:(n + 1) * 512],
                                  start=(k == 0), stop=False)
                    for k in range(DCH):
                        mm = te.matmul(ps[:], l2[:, k, m * 128:(m + 1) * 128],
                                       r2[:, k, n * 512:(n + 1) * 512],
                                       start=False, stop=(k == DCH - 1))
                    p.inc(mm, f"{tag}pe", 1)

            @b2.vector
            def _(v):
                for t, (m, n) in enumerate(sched):
                    wge(v, s_pe, t + 1)
                    if scale is None:
                        last = v.tensor_copy(
                            out_sb[:, m, n * 512:(n + 1) * 512], psl[t % 4][:])
                    else:
                        last = v.tensor_scalar(
                            out_sb[:, m, n * 512:(n + 1) * 512], psl[t % 4][:],
                            scale, None, ALU.mult)
                    p.inc(last, f"{tag}dv", 1)
        pstk.close()

    p.sem("l1fsn")
    with nc.Block() as b:
        @b.vector
        def _(v):
            p.inc(v.engine_nop(), "l1fsn", 1)

    dual_gemm("l1fp", wrb, cb, wib, sneg, pb)
    dual_gemm("l1fq", wrb, sb_, wib, cb, qb)
    dual_gemm("l1fm", cb, pb, sb_, qb, mb, scale=1.0 / DM)
    s1.close()
    if stage <= 1:
        spill(p, "dbg", mb)
        p.stk.close()
        return p

    s3 = contextlib.ExitStack()
    s_bf = p.sb("s_bf", [128, DCH, TOK], BF16, s3)
    sq_bf = p.sb("sq_bf", [128, DCH, TOK], BF16, s3)
    sfp = [p.sb(f"sfp{i}", [128, TOK], F32, s3) for i in range(2)]
    xres = [p.sb(f"xres{i}", [128, TOK], F32, s3) for i in range(2)]
    _emit_proj_res(p, "proj", mb, xbf, xin, s_hbm, s_bf, sq_bf, sfp, xres,
                   bias_sb=biasb)
    emit_postln(p, "ln", s_hbm, s_bf, sq_bf, xout)
    s3.close()
    p.stk.close()
    return p


def _dft_tables():
    k = np.arange(DM, dtype=np.float64)
    ang = 2.0 * np.pi * np.outer(k, k) / DM
    return (np.cos(ang).astype(np.float32), np.sin(ang).astype(np.float32))


def prep_l1f(inputs, xfm_percore):
    C, S = _dft_tables()
    base = {
        "wrT": np.ascontiguousarray(np.asarray(inputs["L1_ffn_w_freq_real"]).T),
        "wiT": np.ascontiguousarray(np.asarray(inputs["L1_ffn_w_freq_imag"]).T),
        "cmat": C, "smat": S,
        "bias": np.ascontiguousarray(
            np.asarray(inputs["L1_ffn_bias"]).reshape(DCH, 128).T),
        "c_ones": HC["onescol"],
    }
    maps = []
    for b in range(NCORE):
        m = dict(base)
        m["xin"] = xfm_percore[b]
        maps.append(m)
    return maps


def emit_group_attention(p, tag, qgT, kgT, vgT, o2sb, ntiles, dh, gmask,
                         semtag=None, stage=99, prevt=False):
    nc = p.nc
    st_ = semtag or tag
    sd = getattr(p, "att_side", "left")
    astk = contextlib.ExitStack()
    KD = min(dh, 128)
    nkc = (dh + 127) // 128
    nm = p.sb(f"{tag}nm", [128, ntiles], F32, astk, side=sd)
    sm = p.sb(f"{tag}sm", [128, ntiles], F32, astk, side=sd)
    rca = p.sb(f"{tag}rca", [128, ntiles], F32, astk, side=sd)
    junk = p.sb(f"{tag}junk", [128, 128], BF16, astk, side=sd)
    cnb = p.sb(f"{tag}cnb", [128, 1], F32, astk, side=sd)
    ncnb = p.sb(f"{tag}ncnb", [128, 1], F32, astk, side=sd)
    minb = p.sb(f"{tag}minb", [128, 1], BF16, astk, side=sd)
    c1b = p.sb(f"{tag}c1b", [1, 128], F32, astk, side=sd)
    c1c = p.sb(f"{tag}c1c", [1, 1], BF16, astk, side=sd)
    attb = [p.sb(f"{tag}attb{i}", [128, 128], BF16, astk, side=sd)
            for i in range(2)]
    vt = [p.sb(f"{tag}vt{i}", [128, dh], BF16, astk, side=sd)
          for i in range(2)]
    idsb, onesr = p.idsb, p.onesr

    pstk = contextlib.ExitStack()
    psA = [p.ps(f"{tag}psA{i}", [128, 128], F32, stk=pstk) for i in range(2)]
    pA, vA, aA = f"{st_}pA", f"{st_}vA", f"{st_}aA"
    sA_p, sA_v, sA_a = p.snap(pA), p.snap(vA), p.snap(aA)
    with nc.Block() as b:
        @b.tensor
        def _(te):
            p.waitall(te)
            for t in range(ntiles):
                if t >= 2:
                    wge(te, sA_a, t - 1)
                for kc in range(nkc):
                    mm = te.matmul(psA[t % 2][:],
                                   qgT[0:KD, kc, t * 128:(t + 1) * 128],
                                   kgT[0:KD, kc, t * 128:(t + 1) * 128],
                                   start=(kc == 0), stop=(kc == nkc - 1))
                p.inc(mm, pA, 1)

        @b.vector
        def _(v):
            for t in range(ntiles):
                wge(v, sA_p, t + 1)
                p.chain(v, v.tensor_tensor(psA[t % 2][:], psA[t % 2][:],
                                           gmask[:], ALU.add))
                last = v.reduce_max(nm[:, t:t + 1], psA[t % 2][:],
                                    axis=AX.X, negate=True)
                p.inc(last, vA, 1)

        @b.scalar
        def _(sc):
            for t in range(ntiles):
                wge(sc, sA_v, t + 1)
                if t >= 1:
                    wge(sc, sA_a, t)
                ac = sc.activation(junk[:], psA[t % 2][:], AF.Exp,
                                   bias=nm[:, t:t + 1],
                                   accum_out=sm[:, t:t + 1])
                p.inc(ac, aA, 1)
    pstk.close()
    if stage <= 51:
        spill(p, "dbgnm" + tag, nm)
        astk.close()
        return

    pstk = contextlib.ExitStack()
    psC = p.ps(f"{tag}psC", [1, 128], BF16, stk=pstk)
    psC2 = p.ps(f"{tag}psC2", [128, 1], F32, stk=pstk)
    with nc.Block() as b:
        @b.vector
        def _(v):
            p.inc(v.tensor_reduce(minb[:], nm[:], axis=AX.X, op=ALU.min),
                  f"{st_}c1", 1)

        @b.tensor
        def _(te):
            te.wait_ge(p.sem(f"{st_}c1"), p.cnt[f"{st_}c1"])
            p.inc(te.transpose(psC[:], minb[:], idsb[:]), f"{st_}c2", 1)

        @b.vector
        def _(v):
            v.wait_ge(p.sem(f"{st_}c2"), p.cnt[f"{st_}c2"])
            p.chain(v, v.tensor_copy(c1b[:], psC[:]))
            p.inc(v.tensor_reduce(c1c[:], c1b[:], axis=AX.X, op=ALU.min),
                  f"{st_}c3", 1)

        @b.tensor
        def _(te):
            te.wait_ge(p.sem(f"{st_}c3"), p.cnt[f"{st_}c3"])
            p.inc(te.matmul(psC2[:], onesr[:], c1c[:], start=True, stop=True),
                  f"{st_}c4", 1)

        @b.vector
        def _(v):
            v.wait_ge(p.sem(f"{st_}c4"), p.cnt[f"{st_}c4"])
            p.chain(v, v.tensor_copy(cnb[:], psC2[:]))
            p.inc(v.tensor_scalar_mul(ncnb[:], cnb[:], -1.0), f"{st_}c5", 1)

        @b.scalar
        def _(sc):
            sc.wait_ge(p.sem(f"{st_}c5"), p.cnt[f"{st_}c5"])
            p.inc(sc.activation(rca[:], nm[:], AF.Exp, bias=ncnb[:, 0:1]),
                  f"{st_}c6", 1)

        @b.vector
        def _(v):
            v.wait_ge(p.sem(f"{st_}c6"), p.cnt[f"{st_}c6"])
            p.chain(v, v.reciprocal(sm[:], sm[:]))
            v.tensor_tensor(rca[:], rca[:], sm[:], ALU.mult)
    pstk.close()
    if stage <= 52:
        spill(p, "dbgrca" + tag, rca)
        astk.close()
        return

    vtall = None
    if prevt:
        vtall = p.sb(f"{tag}vtall", [128, ntiles, dh], BF16, astk, side=sd)
        pstk = contextlib.ExitStack()
        psVt2 = [p.ps(f"{tag}psW{i}", [128, dh], BF16, stk=pstk)
                 for i in range(2)]
        sV_p, sV_d = p.snap(f"{st_}Vp"), p.snap(f"{st_}Vd")
        with nc.Block() as b:
            @b.tensor
            def _(te):
                for t in range(ntiles):
                    if t >= 2:
                        wge(te, sV_d, t - 1)
                    for kc in range(nkc):
                        mm = te.transpose(
                            psVt2[t % 2][:, kc * 128:kc * 128 + KD],
                            vgT[0:KD, kc, t * 128:(t + 1) * 128],
                            idsb[0:KD, 0:KD])
                    p.inc(mm, f"{st_}Vp", 1)

            @b.vector
            def _(v):
                for t in range(ntiles):
                    wge(v, sV_p, t + 1)
                    last = v.tensor_copy(vtall[:, t, :], psVt2[t % 2][:])
                    p.inc(last, f"{st_}Vd", 1)
        pstk.close()

    pstk = contextlib.ExitStack()
    psT = [p.ps(f"{tag}psT{i}", [128, 128], F32, stk=pstk) for i in range(2)]
    psVt = [p.ps(f"{tag}psVt{i}", [128, dh], BF16, stk=pstk)
            for i in range(2)] if not prevt else []
    psAV = [p.ps(f"{tag}psAV{i}", [128, dh], F32, stk=pstk) for i in range(2)]
    pB, vB, aB, pVt, vVt, pAV, vE = (f"{st_}pB", f"{st_}vB", f"{st_}aB",
                                     f"{st_}pVt", f"{st_}vVt", f"{st_}pAV",
                                     f"{st_}vE")
    sB_p, sB_v, sB_a = p.snap(pB), p.snap(vB), p.snap(aB)
    sB_pVt, sB_vVt, sB_pAV, sB_vE = (p.snap(pVt), p.snap(vVt), p.snap(pAV),
                                     p.snap(vE))
    with nc.Block() as b:
        @b.tensor
        def _(te):
            for t in range(ntiles):
                if t >= 2:
                    wge(te, sB_a, t - 1)
                for kc in range(nkc):
                    mm = te.matmul(psT[t % 2][:],
                                   kgT[0:KD, kc, t * 128:(t + 1) * 128],
                                   qgT[0:KD, kc, t * 128:(t + 1) * 128],
                                   start=(kc == 0), stop=(kc == nkc - 1))
                p.inc(mm, pB, 1)
                if not prevt:
                    if t >= 2:
                        wge(te, sB_vVt, t - 1)
                    for kc in range(nkc):
                        mm = te.transpose(
                            psVt[t % 2][:, kc * 128:kc * 128 + KD],
                            vgT[0:KD, kc, t * 128:(t + 1) * 128],
                            idsb[0:KD, 0:KD])
                    p.inc(mm, pVt, 1)
                if t >= 1:
                    wge(te, sB_a, t)
                    if not prevt:
                        wge(te, sB_vVt, t)
                    if t >= 3:
                        wge(te, sB_vE, t - 2)
                    rhsv = vtall[:, t - 1, :] if prevt else vt[(t - 1) % 2][:]
                    mm = te.matmul(psAV[(t - 1) % 2][:], attb[(t - 1) % 2][:],
                                   rhsv, start=True, stop=True)
                    p.inc(mm, pAV, 1)
            wge(te, sB_a, ntiles)
            if not prevt:
                wge(te, sB_vVt, ntiles)
            if ntiles >= 2:
                wge(te, sB_vE, ntiles - 1)
            rhsv = vtall[:, ntiles - 1, :] if prevt else vt[(ntiles - 1) % 2][:]
            mm = te.matmul(psAV[(ntiles - 1) % 2][:], attb[(ntiles - 1) % 2][:],
                           rhsv, start=True, stop=True)
            p.inc(mm, pAV, 1)

        @b.vector
        def _(v):
            for t in range(ntiles):
                wge(v, sB_p, t + 1)
                p.chain(v, v.tensor_tensor(psT[t % 2][:], psT[t % 2][:],
                                           gmask[:], ALU.add))
                p.inc(v.engine_nop(), vB, 1)
                if not prevt:
                    wge(v, sB_pVt, t + 1)
                    if t >= 2:
                        wge(v, sB_pAV, t - 1)
                    last = v.tensor_copy(vt[t % 2][:], psVt[t % 2][:])
                    p.inc(last, vVt, 1)
                if t >= 1:
                    wge(v, sB_pAV, t)
                    last = v.tensor_scalar(o2sb[:, t - 1, :],
                                           psAV[(t - 1) % 2][:],
                                           rca[:, t - 1:t], None, ALU.mult)
                    p.inc(last, vE, 1)
            wge(v, sB_pAV, ntiles)
            last = v.tensor_scalar(o2sb[:, ntiles - 1, :],
                                   psAV[(ntiles - 1) % 2][:],
                                   rca[:, ntiles - 1:ntiles], None, ALU.mult)
            p.inc(last, vE, 1)

        @b.scalar
        def _(sc):
            for t in range(ntiles):
                wge(sc, sB_v, t + 1)
                if t >= 2:
                    wge(sc, sB_pAV, t - 1)
                ac = sc.activation(attb[t % 2][:], psT[t % 2][:], AF.Exp,
                                   bias=cnb[:, 0:1])
                p.inc(ac, aB, 1)
    pstk.close()
    astk.close()


def build_l2a(stage=99):
    p = P("l2a")
    nc = p.nc
    xin = p.dram_in("xin", [DM, TOK])
    wqT = p.dram_in("wqT", [DM, DM])
    wkT = p.dram_in("wkT", [DM, DM])
    wvT = p.dram_in("wvT", [DM, DM])
    woT = p.dram_in("woT", [DM, DM])
    bqr = p.dram_in("bqr", [1, DM])
    bkr = p.dram_in("bkr", [1, DM])
    bvr = p.dram_in("bvr", [1, DM])
    bo = p.dram_in("bo", [128, DCH])
    c_id = p.dram_in("c_id", [128, 128])
    c_ones = p.dram_in("c_ones", [128, 128])
    c_onesr = p.dram_in("c_onesr", [1, 128])
    c_gm16 = p.dram_in("c_gm16", [128, 128])
    xout = p.dram_out("xout", [DM, TOK])
    s_hbm = p.dram_tmp("s_hbm", [DM, TOK])
    xscr_t = nc.dram_tensor("xscr", [TOK, DM], BF16, kind="ExternalOutput")
    xscr = xscr_t.ap()

    p.idsb = p.sb("idsb", [128, 128], BF16)
    p.onesr = p.sb("onesr", [1, 128], BF16)
    p.onescol = p.sb("onescol", [128, 128], BF16)
    gm16 = p.sb("gm16", [128, 128], F32)
    bob = p.sb("bob", [128, DCH], F32)
    brow = [p.sb(f"brow{i}", [1, DM], BF16) for i in range(3)]
    onestok = p.sb("onestok", [1, 128], BF16)

    NT2 = NQT * 16

    s1b = contextlib.ExitStack()
    qvg = p.sb("qvg", [128, 1, NT2 * 128], BF16, s1b)
    kg = p.sb("kg", [128, 1, NT2 * 128], BF16, s1b)
    qgT = qvg[64:128]
    vgT = qvg[0:64]
    kgT = kg[64:128]

    s1 = contextlib.ExitStack()
    xbf = p.sb("xbf", [128, DCH, TOK], BF16, s1)
    wb = p.sb("wb", [128, DCH, DM], BF16, s1)
    ptm = p.sb("ptm", [128, 2, DM], BF16, s1)

    with nc.Block() as b:
        @b.gpsimd
        def _(g):
            cast_load(p, g, xbf, xin, DCH)
            p.dma(g, p.idsb[:], c_id)
            p.dma(g, p.onesr[:], c_onesr)
            p.dma(g, p.onescol[:], c_ones)
            p.dma(g, brow[0][:], bqr)
            p.dma(g, brow[1][:], bkr)
            p.dma(g, brow[2][:], bvr)
            g.memset(onestok[:], 1.0)

        @b.sync
        def _(sy):
            p.dma(sy, gm16[:], c_gm16)
            p.dma(sy, bob[:], bo)

    def build_gT(widx, wsrc, gT, scale):
        with nc.Block() as b2:
            @b2.gpsimd
            def _(g):
                cast_load(p, g, wb, wsrc, DCH)
        upper = (widx != 2)
        lo = 64 if upper else 0
        pstk = contextlib.ExitStack()
        psv = [p.ps(f"g{widx}ps{i}", [128, 512], F32, stk=pstk)
               for i in range(4)]
        pstF = [p.ps(f"g{widx}pt{i}", [128, 128], BF16, stk=pstk)
                for i in range(2)]
        pst = [t[lo:lo + 64, :] for t in pstF]
        s_pe, s_dv = p.snap("gtpe"), p.snap("gtdv")
        s_pe2, s_dv2 = p.snap("gtpe2"), p.snap("gtdv2")
        gview = gT.rearrange("p c (a i) -> p c a i", i=16)

        with nc.Block() as b2:
            @b2.tensor
            def _(te):
                p.waitall(te)
                j = 0
                for t in range(NQT):
                    for n in range(2):
                        if 2 * t + n >= 4:
                            wge(te, s_dv, 2 * t + n - 3)
                        for k in range(DCH):
                            te.matmul(psv[(2 * t + n) % 4][:],
                                      xbf[:, k, t * 128:(t + 1) * 128],
                                      wb[:, k, n * 512:(n + 1) * 512],
                                      start=(k == 0), stop=False)
                        mm = te.matmul(psv[(2 * t + n) % 4][:], onestok[:],
                                       brow[widx][:, n * 512:(n + 1) * 512],
                                       start=False, stop=True)
                        p.inc(mm, "gtpe", 1)
                    if t >= 1:
                        wge(te, s_dv, 2 * t)
                        for hh in range(16):
                            if j >= 2:
                                wge(te, s_dv2, j - 1)
                            mm = te.transpose(
                                pst[j % 2][:],
                                ptm[:, (t - 1) % 2, hh * 64:(hh + 1) * 64],
                                p.idsb[:])
                            p.inc(mm, "gtpe2", 1)
                            j += 1
                wge(te, s_dv, 2 * NQT)
                for hh in range(16):
                    if j >= 2:
                        wge(te, s_dv2, j - 1)
                    mm = te.transpose(pst[j % 2][:],
                                      ptm[:, (NQT - 1) % 2,
                                          hh * 64:(hh + 1) * 64],
                                      p.idsb[:])
                    p.inc(mm, "gtpe2", 1)
                    j += 1

            @b2.vector
            def _(v):
                j = 0
                for t in range(NQT):
                    if t >= 2:
                        wge(v, s_pe2, 16 * (t - 1))
                    for n in range(2):
                        wge(v, s_pe, 2 * t + n + 1)
                        if scale is None:
                            last = v.tensor_copy(
                                ptm[:, t % 2, n * 512:(n + 1) * 512],
                                psv[(2 * t + n) % 4][:])
                        else:
                            last = v.tensor_scalar(
                                ptm[:, t % 2, n * 512:(n + 1) * 512],
                                psv[(2 * t + n) % 4][:], scale, None,
                                ALU.mult)
                        p.inc(last, "gtdv", 1)
                    if t >= 1:
                        for hh in range(16):
                            wge(v, s_pe2, j + 1)
                            last = v.tensor_copy(
                                gview[0:64, 0,
                                      (t - 1) * 128:t * 128, hh],
                                pst[j % 2][:, :])
                            p.inc(last, "gtdv2", 1)
                            j += 1
                for hh in range(16):
                    wge(v, s_pe2, j + 1)
                    last = v.tensor_copy(
                        gview[0:64, 0, (NQT - 1) * 128:NQT * 128, hh],
                        pst[j % 2][:])
                    p.inc(last, "gtdv2", 1)
                    j += 1
        pstk.close()

    build_gT(0, wqT, qgT, 1.0 / 8.0)
    build_gT(1, wkT, kgT, None)
    build_gT(2, wvT, vgT, None)
    if stage <= 4:
        spill(p, "dbgq", qgT)
        spill(p, "dbgv", vgT)
        s1.close(); s1b.close(); p.stk.close()
        return p
    s1.close()

    s1c = contextlib.ExitStack()
    o2sb = p.sb("o2sb", [128, NT2, 64], BF16, s1c, side="right")
    p.att_side = "right"
    emit_group_attention(p, "gat", qgT, kgT, vgT, o2sb, NT2, 64, gm16)
    if stage <= 7:
        spill(p, "dbg", o2sb)
        s1c.close(); s1b.close(); p.stk.close()
        return p
    s1b.close()

    with nc.Block() as b:
        @b.sync
        def _(sy):
            for a8 in range(8):
                for par in range(2):
                    srcv = o2sb[a8 * 16:(a8 + 1) * 16, :, :].rearrange(
                        "p (te par) d -> p par te d", par=2)[:, par, :, :]
                    dst = bass.AP(xscr_t, (8 * par + a8) * 64,
                                  [[131072, 16], [1024, 128], [1, 64]])
                    p.dma(sy, dst, srcv, "l2sc")

    s3 = contextlib.ExitStack()
    xscrf = p.sb("xscrf", [128, DCH, TOK], BF16, s3)
    with nc.Block() as b:
        @b.sync
        def _(sy):
            p.waitall(sy, "l2sc")
            for c in range(DCH):
                p.dma(sy, xscrf[:, c, :],
                      xscr[:, c * 128:(c + 1) * 128], "l2tr", transpose=True)
    s1c.close()

    wpb = p.sb("wpb", [128, DCH, DM], BF16, s3)
    s_bf = p.sb("s_bf", [128, DCH, TOK], BF16, s3)
    sq_bf = p.sb("sq_bf", [128, DCH, TOK], BF16, s3)
    sfp = [p.sb(f"sfp{i}", [128, TOK], F32, s3) for i in range(2)]
    xres = [p.sb(f"xres{i}", [128, TOK], F32, s3) for i in range(2)]
    with nc.Block() as b:
        @b.gpsimd
        def _(g):
            cast_load(p, g, wpb, woT, DCH)
    _emit_proj_res(p, "proj", wpb, xscrf, xin, s_hbm, s_bf, sq_bf, sfp, xres,
                   bias_sb=bob)
    emit_postln(p, "ln", s_hbm, s_bf, sq_bf, xout)
    s3.close()
    p.stk.close()
    return p


def _fmvec(v):
    return np.ascontiguousarray(np.asarray(v).reshape(-1, 128).T)


def prep_l2a(inputs, xfm_percore):
    gm = np.zeros((128, 128), np.float32) + NEG
    for a in range(8):
        gm[a * 16:(a + 1) * 16, a * 16:(a + 1) * 16] = 0.0
    base = {
        "wqT": np.ascontiguousarray(np.asarray(inputs["L2_attn_Wq"]).T),
        "wkT": np.ascontiguousarray(np.asarray(inputs["L2_attn_Wk"]).T),
        "wvT": np.ascontiguousarray(np.asarray(inputs["L2_attn_Wv"]).T),
        "woT": np.ascontiguousarray(np.asarray(inputs["L2_attn_Wo"]).T),
        "bqr": np.ascontiguousarray(np.asarray(inputs["L2_attn_bq"]).reshape(1, DM)),
        "bkr": np.ascontiguousarray(np.asarray(inputs["L2_attn_bk"]).reshape(1, DM)),
        "bvr": np.ascontiguousarray(np.asarray(inputs["L2_attn_bv"]).reshape(1, DM)),
        "bo": _fmvec(inputs["L2_attn_bo"]),
        "c_id": HC["ident"], "c_ones": HC["onescol"], "c_onesr": HC["onesrow"],
        "c_gm16": gm,
    }
    maps = []
    for b in range(NCORE):
        m = dict(base)
        m["xin"] = xfm_percore[b]
        maps.append(m)
    return maps


def build_l2fa(stage=99):
    p = P("l2fa")
    nc = p.nc
    xin = p.dram_in("xin", [DM, TOK])
    wipT = p.dram_in("wipT", [DM, 3 * DM])
    wopT = p.dram_in("wopT", [DM, DM])
    bip = p.dram_in("bip", [128, 3 * DCH])
    bop = p.dram_in("bop", [128, DCH])
    ln1g = p.dram_in("ln1g", [128, DCH])
    ln1b = p.dram_in("ln1b", [128, DCH])
    wg = p.dram_in("wg", [DM, 8])
    c_id = p.dram_in("c_id", [128, 128])
    c_ones = p.dram_in("c_ones", [128, 128])
    c_onesr = p.dram_in("c_onesr", [1, 128])
    c_gm8 = p.dram_in("c_gm8", [128, 128])
    xout = p.dram_out("xout", [DM, TOK])
    lgout = p.dram_out("lgout", [TOK, 8])
    s_hbm = p.dram_tmp("s_hbm", [DM, TOK])

    p.idsb = p.sb("idsb", [128, 128], BF16)
    p.onesr = p.sb("onesr", [1, 128], BF16)
    p.onescol = p.sb("onescol", [128, 128], BF16)
    gm8 = p.sb("gm8", [128, 128], F32)
    bipb = p.sb("bipb", [128, 3 * DCH], F32)
    bopb = p.sb("bopb", [128, DCH], F32)
    ln1gb = p.sb("ln1gb", [128, DCH], F32)
    ln1bb = p.sb("ln1bb", [128, DCH], F32)

    s1b = contextlib.ExitStack()
    qf = p.sb("qf", [128, DCH, TOK], BF16, s1b)
    kf = p.sb("kf", [128, DCH, TOK], BF16, s1b)
    vf = p.sb("vf", [128, DCH, TOK], BF16, s1b)

    s1 = contextlib.ExitStack()
    xbf = p.sb("xbf", [128, DCH, TOK], BF16, s1)
    wb = p.sb("wb", [128, DCH, 3 * DM], BF16, s1)

    with nc.Block() as b:
        @b.gpsimd
        def _(g):
            cast_load(p, g, xbf, xin, DCH)
            cast_load(p, g, wb, wipT, DCH)
            p.dma(g, p.idsb[:], c_id)
            p.dma(g, p.onesr[:], c_onesr)
            p.dma(g, p.onescol[:], c_ones)

        @b.sync
        def _(sy):
            p.dma(sy, gm8[:], c_gm8)
            p.dma(sy, bipb[:], bip)
            p.dma(sy, bopb[:], bop)
            p.dma(sy, ln1gb[:], ln1g)
            p.dma(sy, ln1bb[:], ln1b)

    def qkv_evac(v, m, n, ps):
        if m < 8:
            return v.tensor_scalar(qf[:, m, n * 512:(n + 1) * 512], ps[:],
                                   bipb[:, m:m + 1], 1.0 / 16.0,
                                   ALU.add, ALU.mult)
        if m < 16:
            return v.tensor_scalar(kf[:, m - 8, n * 512:(n + 1) * 512], ps[:],
                                   bipb[:, m:m + 1], None, ALU.add)
        return v.tensor_scalar(vf[:, m - 16, n * 512:(n + 1) * 512], ps[:],
                               bipb[:, m:m + 1], None, ALU.add)

    emit_linear_fm(p, "l2fqkv", wb, xbf, DCH, 24, qkv_evac)
    s1.close()
    if stage <= 6:
        spill(p, "dbg", qf)
        spill(p, "dbgv", vf)
        s1b.close(); p.stk.close()
        return p

    s1c = contextlib.ExitStack()
    otm = p.sb("otm", [128, NQT, DM], BF16, s1c, side="right")
    p.att_side = "right"
    nheads = 1 if stage in (51, 52, 65) else 4
    for h in range(nheads):
        emit_group_attention(
            p, f"gm{h}", qf[:, 2 * h:2 * h + 2, :], kf[:, 2 * h:2 * h + 2, :],
            vf[:, 2 * h:2 * h + 2, :],
            otm.rearrange("p t (h d) -> p t h d", h=4)[:, :, h, :],
            NQT, 256, gm8, semtag="gmh",
            stage=(stage if stage in (51, 52) else 99), prevt=True)
    if stage in (51, 52):
        s1c.close(); s1b.close(); p.stk.close()
        return p
    if stage <= 7:
        spill(p, "dbg", otm)
        s1c.close(); s1b.close(); p.stk.close()
        return p
    s1b.close()

    s3 = contextlib.ExitStack()
    obf = p.sb("obf", [128, DCH, TOK], BF16, s3)
    _emit_tm2fm(p, "o2f", otm, obf, DCH)
    s1c.close()
    if stage <= 8:
        spill(p, "dbg8", obf)
        s3.close(); s1b.close(); p.stk.close()
        return p

    wpb = p.sb("wpb", [128, DCH, DM], BF16, s3)
    s_bf = p.sb("s_bf", [128, DCH, TOK], BF16, s3)
    sq_bf = p.sb("sq_bf", [128, DCH, TOK], BF16, s3)
    sfp = [p.sb(f"sfp{i}", [128, TOK], F32, s3) for i in range(2)]
    xres = [p.sb(f"xres{i}", [128, TOK], F32, s3) for i in range(2)]
    with nc.Block() as b:
        @b.gpsimd
        def _(g):
            cast_load(p, g, wpb, wopT, DCH)
    _emit_proj_res(p, "proj", wpb, obf, xin, s_hbm, s_bf, sq_bf, sfp, xres,
                   bias_sb=bopb)
    if stage <= 9:
        spill(p, "dbg9", s_bf)
        s3.close(); s1b.close(); p.stk.close()
        return p
    emit_postln(p, "ln", s_hbm, s_bf, sq_bf, xout, gb=(ln1gb, ln1bb))
    s3.close()
    if stage <= 10:
        s1b.close(); p.stk.close()
        return p

    s4 = contextlib.ExitStack()
    rwb = p.sb("rwb", [128, DCH, 8], F32, s4)
    y32 = [p.sb(f"y32{i}", [128, TOK], F32, s4) for i in range(2)]
    lacc = p.sb("lacc", [128, NQT, 8], F32, s4)
    lgsb = p.sb("lgsb", [128, NQT * 8], F32, s4)
    with nc.Block() as b:
        @b.sync
        def _(sy):
            for c in range(DCH):
                p.dma(sy, rwb[:, c, :], wg[c * 128:(c + 1) * 128, :], "rwl")

        @b.gpsimd
        def _(g):
            g.memset(lacc.rearrange("p a b -> p (a b)"), 0.0)
            p.inc(g.engine_nop(), "laz2", 1)

    pstk = contextlib.ExitStack()
    psl = [p.ps(f"l2lg{i}", [128, 8], F32, stk=pstk) for i in range(2)]
    s_pe2, s_dv2, s_xr2 = p.snap("l2lpe"), p.snap("l2ldv"), p.snap("l2lxr")
    s_rwl = p.snap("rwl")
    sched_l = [(c, t) for c in range(DCH) for t in range(NQT)]
    with nc.Block() as b:
        @b.sync
        def _(sy):
            for c in range(DCH):
                if c >= 2:
                    wge(sy, s_dv2, (c - 1) * NQT)
                p.dma(sy, y32[c % 2][:], xout[c * 128:(c + 1) * 128, :],
                      "l2lxr")

        @b.tensor
        def _(te):
            te.wait_ge(p.sem("rwl"), p.cnt["rwl"])
            for i, (c, t) in enumerate(sched_l):
                wge(te, s_xr2, (c + 1) * 16)
                if i >= 2:
                    wge(te, s_dv2, i - 1)
                mm = te.matmul(psl[i % 2][:],
                               y32[c % 2][:, t * 128:(t + 1) * 128],
                               rwb[:, c, :], start=True, stop=True)
                p.inc(mm, "l2lpe", 1)

        @b.vector
        def _(v):
            v.wait_ge(p.sem("laz2"), p.cnt["laz2"])
            for i, (c, t) in enumerate(sched_l):
                wge(v, s_pe2, i + 1)
                last = v.tensor_tensor(lacc[:, t, :], lacc[:, t, :],
                                       psl[i % 2][:], ALU.add)
                p.inc(last, "l2ldv", 1)
    pstk.close()

    with nc.Block() as b:
        @b.vector
        def _(v):
            p.inc(v.tensor_copy(lgsb[:], lacc.rearrange("p a b -> p (a b)")),
                  "lgc", 1)

        @b.sync
        def _(sy):
            sy.wait_ge(p.sem("lgc"), p.cnt["lgc"])
            p.dma(sy, lgout.rearrange("(T q) e -> q T e", q=128),
                  lgsb.rearrange("p (T e) -> p T e", e=8), "lgst")

        @b.gpsimd
        def _(g):
            p.waitall(g, "lgst")
    s4.close()
    p.stk.close()
    return p


def prep_l2fa(inputs, xps_percore):
    base = {
        "wipT": np.ascontiguousarray(np.asarray(inputs["L2_ffn_in_proj_w"]).T),
        "wopT": np.ascontiguousarray(np.asarray(inputs["L2_ffn_out_proj_w"]).T),
        "bip": _fmvec(inputs["L2_ffn_in_proj_b"]),
        "bop": _fmvec(inputs["L2_ffn_out_proj_b"]),
        "ln1g": _fmvec(inputs["L2_ffn_ln1_g"]),
        "ln1b": _fmvec(inputs["L2_ffn_ln1_b"]),
        "wg": np.asarray(inputs["L2_ffn_wg"]),
        "c_id": HC["ident"], "c_ones": HC["onescol"], "c_onesr": HC["onesrow"],
        "c_gm8": HC["gmask8"],
    }
    maps = []
    for j in range(NCORE):
        m = dict(base)
        m["xin"] = xps_percore[j]
        maps.append(m)
    return maps


def build_l2fb(stage=99):
    p = P("l2fb")
    nc = p.nc
    CAP = 2048.0
    xin = p.dram_in("xin", [DM, TOK])
    lgf = p.dram_in("lgf", [16384, 8])
    w1aT = p.dram_in("w1aT", [8, 4096, 1152])
    w2a = p.dram_in("w2a", [8, 4096, DM])
    b2a = p.dram_in("b2a", [8, DM])
    ln2g = p.dram_in("ln2g", [128, DCH])
    ln2b = p.dram_in("ln2b", [128, DCH])
    c_id = p.dram_in("c_id", [128, 128])
    c_ones = p.dram_in("c_ones", [128, 128])
    c_onesr = p.dram_in("c_onesr", [1, 128])
    c_tin = p.dram_in("c_tin", [128, 128])
    c_tex = p.dram_in("c_tex", [128, 128])
    coreid = p.dram_in("coreid", [1, 1])
    xout = p.dram_out("xout", [DM, TOK])
    s_hbm = p.dram_tmp("s_hbm", [DM, TOK])
    wcd_t = nc.dram_tensor("wcd", [8, DM, DM], BF16, kind="ExternalOutput")
    wcd = wcd_t.ap()
    selr_t = nc.dram_tensor("selr", [16384, 8], F32, kind="ExternalOutput")
    selr = selr_t.ap()
    bnc_t = nc.dram_tensor("bnc", [1, 2048], F32, kind="ExternalOutput")
    bnc = bnc_t.ap()
    bcout = p.dram_out("bcout", [8, DM])

    p.idsb = p.sb("idsb", [128, 128], BF16)
    p.onesr = p.sb("onesr", [1, 128], BF16)
    p.onescol = p.sb("onescol", [128, 128], BF16)
    onesr32 = p.sb("onesr32", [1, 128], F32)
    tinb = p.sb("tinb", [128, 128], BF16)
    texb = p.sb("texb", [128, 128], BF16)
    ln2gb = p.sb("ln2gb", [128, DCH], F32)
    ln2bb = p.sb("ln2bb", [128, DCH], F32)
    bc8 = p.sb("bc8", [8, DM], F32)
    bc8b = p.sb("bc8b", [8, DM], BF16)
    b2sb = p.sb("b2sb", [8, DM], F32)

    with nc.Block() as b:
        @b.gpsimd
        def _(g):
            p.dma(g, p.idsb[:], c_id)
            p.dma(g, p.onesr[:], c_onesr)
            p.dma(g, p.onescol[:], c_ones)
            p.dma(g, tinb[:], c_tin)
            p.dma(g, texb[:], c_tex)
            g.memset(onesr32[:], 1.0)

        @b.sync
        def _(sy):
            p.dma(sy, ln2gb[:], ln2g)
            p.dma(sy, ln2bb[:], ln2b)
            p.dma(sy, b2sb[:], b2a)

    castk = contextlib.ExitStack()
    w1sb = p.sb("w1sb", [128, 32, 1152], BF16, castk)
    w2sb = p.sb("w2sb", [128, 32, DM], BF16, castk)
    wcst = p.sb("wcst", [128, DCH, DM], BF16, castk)
    bcrow = p.sb("bcrow", [1, DM], F32, castk)
    for e in range(8):
        with nc.Block() as b:
            @b.gpsimd
            def _(g):
                for k in range(32):
                    p.dma(g, w1sb[:, k, :], w1aT[e, k * 128:(k + 1) * 128, :])
                    p.dma(g, w2sb[:, k, :], w2a[e, k * 128:(k + 1) * 128, :])

        pstk = contextlib.ExitStack()
        psl = [p.ps(f"cl{e}_{i}", [128, 512], F32, stk=pstk) for i in range(4)]
        s_pe, s_dv = p.snap("clpe"), p.snap("cldv")
        sched = [(m, n) for m in range(9) for n in range(2)]
        with nc.Block() as b:
            @b.tensor
            def _(te):
                p.waitall(te)
                for i, (m, n) in enumerate(sched):
                    if i >= 4:
                        wge(te, s_dv, i - 3)
                    for k in range(32):
                        mm = te.matmul(psl[i % 4][:],
                                       w1sb[:, k, m * 128:(m + 1) * 128],
                                       w2sb[:, k, n * 512:(n + 1) * 512],
                                       start=(k == 0), stop=(k == 31))
                    p.inc(mm, "clpe", 1)

            @b.vector
            def _(v):
                for i, (m, n) in enumerate(sched):
                    wge(v, s_pe, i + 1)
                    if m < 8:
                        last = v.tensor_copy(
                            wcst[:, m, n * 512:(n + 1) * 512], psl[i % 4][:])
                    else:
                        last = v.tensor_copy(
                            bcrow[0:1, n * 512:(n + 1) * 512],
                            psl[i % 4][0:1, :])
                    p.inc(last, "cldv", 1)
        pstk.close()

        with nc.Block() as b:
            @b.sync
            def _(sy):
                for c in range(DCH):
                    p.dma(sy, wcd[e, c * 128:(c + 1) * 128, :], wcst[:, c, :],
                          "wcs")
                p.dma(sy, bnc[0:1, 0:DM], bcrow[:], "wcs")
                sy.wait_ge(p.sem("wcs"), p.cnt["wcs"])
                p.dma(sy, bc8[e:e + 1, :], bnc[0:1, 0:DM], "bcl")

            @b.gpsimd
            def _(g):
                p.waitall(g, "bcl")
    castk.close()

    with nc.Block() as b:
        @b.vector
        def _(v):
            p.waitall(v)
            p.chain(v, v.tensor_tensor(bc8[:], bc8[:], b2sb[:], ALU.add))
            v.tensor_copy(bc8b[:], bc8[:])

    if stage <= 1:
        spill(p, "dbgbc", bc8)
        p.stk.close()
        return p

    rstk = contextlib.ExitStack()
    lgsb = p.sb("lgsb", [128, 128, 8], F32, rstk)
    selsb = p.sb("selsb", [128, 128, 8], F32, rstk)
    cums = p.sb("cums", [128, 128, 8], F32, rstk)
    gv = p.sb("gv", [128, 128], F32, rstk)
    m1sb = p.sb("m1sb", [128, 128, 8], BF16, rstk)
    rsc = p.sb("rsc2", [128, 8], F32, rstk)
    junk8 = p.sb("junk8", [128, 8], BF16, rstk)
    totrow = p.sb("totrow", [1, 1024], F32, rstk)
    offrow = p.sb("offrow", [1, 1024], F32, rstk)
    tot128 = p.sb("tot128", [128, 8], BF16, rstk)
    offs128 = p.sb("offs128", [128, 8], F32, rstk)
    sel_loc = p.sb("sel_loc", [128, NQT, 8], F32, rstk)

    with nc.Block() as b:
        @b.sync
        def _(sy):
            for tq in range(8):
                p.dma(sy, lgsb[:, tq * 16:(tq + 1) * 16, :],
                      lgf[tq * 2048:(tq + 1) * 2048, :].rearrange(
                          "(T q) e -> q T e", q=128), "lgl")

    pstk = contextlib.ExitStack()
    psc = [p.ps(f"rc{i}", [128, 8], F32, stk=pstk) for i in range(2)]
    pst1 = [p.ps(f"rt{i}", [1, 8], F32, stk=pstk) for i in range(2)]
    s_rt, s_re, s_rm = p.snap("rrt"), p.snap("rre"), p.snap("rrm")
    s_rp, s_rv = p.snap("rrp"), p.snap("rrv")
    with nc.Block() as b:
        @b.vector
        def _(v):
            p.waitall(v, "lgl")
            for t in range(128):
                lg = lgsb[:, t, :]
                last = v.reduce_max(rsc[:, 0:1], lg, axis=AX.X, negate=True)
                p.inc(last, "rrt", 1)
                wge(v, s_re, t + 1)
                p.chain(v, v.reciprocal(gv[:, t:t + 1], rsc[:, 1:2]))
                p.chain(v, v.tensor_scalar_mul(rsc[:, 3:4], rsc[:, 0:1], -1.0))
                p.chain(v, v.tensor_tensor(
                    selsb[:, t, :], lg, rsc[:, 3:4].broadcast_to([128, 8]),
                    ALU.is_ge))
                last = v.tensor_copy(m1sb[:, t, :], selsb[:, t, :])
                p.inc(last, "rrm", 1)
                wge(v, s_rp, t + 1)
                p.chain(v, v.tensor_copy(cums[:, t, :], psc[t % 2][:]))
                last = v.tensor_copy(totrow[0:1, t * 8:(t + 1) * 8],
                                     pst1[t % 2][0:1, :])
                p.inc(last, "rrv", 1)

        @b.scalar
        def _(sc):
            for t in range(128):
                wge(sc, s_rt, t + 1)
                ac = sc.activation(junk8[:], lgsb[:, t, :], AF.Exp,
                                   bias=rsc[:, 0:1], accum_out=rsc[:, 1:2])
                p.inc(ac, "rre", 1)

        @b.tensor
        def _(te):
            for t in range(128):
                wge(te, s_rm, t + 1)
                if t >= 2:
                    wge(te, s_rv, t - 1)
                te.matmul(psc[t % 2][:], tinb[:], m1sb[:, t, :],
                          start=True, stop=True)
                mm = te.matmul(pst1[t % 2][:], p.onescol[:, 0:1],
                               m1sb[:, t, :], start=True, stop=True)
                p.inc(mm, "rrp", 1)
    pstk.close()

    pstk = contextlib.ExitStack()
    psof = p.ps("psof", [128, 8], F32, stk=pstk)
    with nc.Block() as b:
        @b.sync
        def _(sy):
            p.dma(sy, bnc[0:1, 0:1024], totrow[:], "t2a")

        @b.gpsimd
        def _(g):
            p.waitall(g, "t2a")
            p.dma(g, tot128[:], bnc[0:1, 0:1024].rearrange("a (T e) -> (a T) e", e=8),
                  "t2b")

        @b.tensor
        def _(te):
            p.waitall(te, "t2b")
            p.inc(te.matmul(psof[:], texb[:], tot128[:], start=True,
                            stop=True), "t2p", 1)

        @b.vector
        def _(v):
            v.wait_ge(p.sem("t2p"), p.cnt["t2p"])
            p.inc(v.tensor_copy(offs128[:], psof[:]), "t2v", 1)

    with nc.Block() as b:
        @b.sync
        def _(sy):
            p.dma(sy, bnc[0:1, 0:1024].rearrange("a (T e) -> (a T) e", e=8),
                  offs128[:], "t2c")
            sy.wait_ge(p.sem("t2c"), p.cnt["t2c"])
            p.dma(sy, offrow[:], bnc[0:1, 0:1024], "t2d")

    pstk2 = contextlib.ExitStack()
    psob = [p.ps(f"ro{i}", [128, 8], F32, stk=pstk2) for i in range(2)]
    s_ob, s_sv = p.snap("rob"), p.snap("rsv")
    with nc.Block() as b:
        @b.tensor
        def _(te):
            p.waitall(te, "t2d")
            for t in range(128):
                if t >= 2:
                    wge(te, s_sv, t - 1)
                mm = te.matmul(psob[t % 2][:], onesr32[:],
                               offrow[0:1, t * 8:(t + 1) * 8],
                               start=True, stop=True)
                p.inc(mm, "rob", 1)

        @b.vector
        def _(v):
            for t in range(128):
                wge(v, s_ob, t + 1)
                p.chain(v, v.tensor_tensor(cums[:, t, :], cums[:, t, :],
                                           psob[t % 2][:], ALU.add))
                p.chain(v, v.tensor_scalar_add(cums[:, t, :], cums[:, t, :],
                                               -1.0))
                p.chain(v, v.tensor_tensor(psob[t % 2][:], selsb[:, t, :],
                                           cums[:, t, :], ALU.mult))
                p.chain(v, v.reduce_sum(rsc[:, 4:5], psob[t % 2][:],
                                        axis=AX.X))
                p.chain(v, v.tensor_single_scalar(rsc[:, 5:6], rsc[:, 4:5],
                                                  CAP, ALU.is_lt))
                p.chain(v, v.tensor_tensor(rsc[:, 6:7], rsc[:, 5:6],
                                           gv[:, t:t + 1], ALU.mult))
                last = v.tensor_scalar(selsb[:, t, :], selsb[:, t, :],
                                       rsc[:, 6:7], None, ALU.mult)
                p.inc(last, "rsv", 1)
    pstk2.close()
    pstk.close()

    with nc.Block() as b:
        @b.sync
        def _(sy):
            sy.wait_ge(p.sem("rsv"), p.cnt["rsv"])
            p.dma(sy, selr.rearrange("(T q) e -> q T e", q=128), selsb[:],
                  "sst")
            p.dma(sy, bcout, bc8[:], "sst")

        @b.gpsimd
        def _(g):
            p.waitall(g, "sst")
    rstk.close()
    p.stk.close()
    return p


def build_l2fc(stage=99):
    p = P("l2fc")
    nc = p.nc
    xin = p.dram_in("xin", [DM, TOK])
    x5in = p.dram_in("x5in", [DM, TOK])
    selin = p.dram_in("selin", [128, NQT * 8])
    wcd = p.dram_in("wcd", [8, DM, DM], BF16)
    bca = p.dram_in("bca", [8, DM])
    ln2g = p.dram_in("ln2g", [128, DCH])
    ln2b = p.dram_in("ln2b", [128, DCH])
    c_id = p.dram_in("c_id", [128, 128])
    c_idf = p.dram_in("c_idf", [128, 128])
    c_ones = p.dram_in("c_ones", [128, 128])
    xout = p.dram_out("xout", [DM, TOK])
    s_hbm = p.dram_tmp("s_hbm", [DM, TOK])

    p.idsb = p.sb("idsb", [128, 128], BF16)
    p.onescol = p.sb("onescol", [128, 128], BF16)
    idf32 = p.sb("idf32", [128, 128], F32)
    ln2gb = p.sb("ln2gb", [128, DCH], F32)
    ln2bb = p.sb("ln2bb", [128, DCH], F32)
    estk = contextlib.ExitStack()
    sel_loc = p.sb("sel_loc", [128, NQT, 8], F32, estk)
    selbf = p.sb("selbf", [128, NQT, 8], BF16, estk)
    selT = p.sb("selT", [8, NQT * 128], BF16, estk)
    bc8b = p.sb("bc8b", [8, DM], BF16, estk)
    xbf = p.sb("xbf", [128, DCH, TOK], BF16, estk)
    accstk = contextlib.ExitStack()
    acc = p.sb("acc", [128, NQT, DM], F32, accstk, side="right")
    wcb = [p.sb(f"wcb{i}", [128, DCH, DM], BF16, estk) for i in range(2)]

    with nc.Block() as b:
        @b.gpsimd
        def _(g):
            cast_load(p, g, xbf, xin, DCH)
            p.dma(g, p.idsb[:], c_id)
            p.dma(g, p.onescol[:], c_ones)
            p.dma(g, bc8b[:], bca)
            p.dma(g, selbf.rearrange("p a b -> p (a b)"), selin)

        @b.sync
        def _(sy):
            p.dma(sy, idf32[:], c_idf)
            p.dma(sy, sel_loc.rearrange("p a b -> p (a b)"), selin)
            p.dma(sy, ln2gb[:], ln2g)
            p.dma(sy, ln2bb[:], ln2b)
            for c in range(DCH):
                p.dma(sy, wcb[0][:, c, :], wcd[0, c * 128:(c + 1) * 128, :],
                      "wc0")

    pstk = contextlib.ExitStack()
    pstl = [p.ps(f"bi{i}", [8, 128], BF16, stk=pstk) for i in range(2)]
    psb = [p.ps(f"bb{i}", [128, 512], F32, stk=pstk) for i in range(2)]
    s_tp, s_tv = p.snap("bitp"), p.snap("bitv")
    s_bp, s_bv = p.snap("bibp"), p.snap("bibv")
    with nc.Block() as b:
        @b.tensor
        def _(te):
            p.waitall(te)
            for t in range(NQT):
                if t >= 2:
                    wge(te, s_tv, t - 1)
                mm = te.transpose(pstl[t % 2][:], selbf[:, t, :], p.idsb[:])
                p.inc(mm, "bitp", 1)

        @b.vector
        def _(v):
            for t in range(NQT):
                wge(v, s_tp, t + 1)
                last = v.tensor_copy(selT[:, t * 128:(t + 1) * 128],
                                     pstl[t % 2][:])
                p.inc(last, "bitv", 1)

    with nc.Block() as b:
        @b.tensor
        def _(te):
            i = 0
            for t in range(NQT):
                for n in range(2):
                    if i >= 2:
                        wge(te, s_bv, i - 1)
                    mm = te.matmul(psb[i % 2][:],
                                   selT[:, t * 128:(t + 1) * 128],
                                   bc8b[:, n * 512:(n + 1) * 512],
                                   start=True, stop=True)
                    p.inc(mm, "bibp", 1)
                    i += 1

        @b.vector
        def _(v):
            i = 0
            for t in range(NQT):
                for n in range(2):
                    wge(v, s_bp, i + 1)
                    last = v.tensor_copy(acc[:, t, n * 512:(n + 1) * 512],
                                         psb[i % 2][:])
                    p.inc(last, "bibv", 1)
                    i += 1
    pstk.close()

    for e in range(8):
        pstk = contextlib.ExitStack()
        psd = [p.ps(f"ap{e}_{i}", [128, 512], F32, stk=pstk) for i in range(4)]
        s_pe, s_dv = p.snap("appe"), p.snap("apdv")
        wcur = wcb[e % 2]
        wnxt = wcb[(e + 1) % 2]
        sched = [(m, n) for m in range(NQT) for n in range(2)]
        with nc.Block() as b:
            if e < 7:
                @b.sync
                def _(sy):
                    for c in range(DCH):
                        p.dma(sy, wnxt[:, c, :],
                              wcd[e + 1, c * 128:(c + 1) * 128, :],
                              f"wc{e + 1}")

            @b.tensor
            def _(te):
                te.wait_ge(p.sem(f"wc{e}"), p.cnt[f"wc{e}"])
                for i, (m, n) in enumerate(sched):
                    if i >= 4:
                        wge(te, s_dv, i - 3)
                    for k in range(DCH):
                        mm = te.matmul(psd[i % 4][:],
                                       xbf[:, k, m * 128:(m + 1) * 128],
                                       wcur[:, k, n * 512:(n + 1) * 512],
                                       start=(k == 0), stop=(k == DCH - 1))
                    p.inc(mm, "appe", 1)

            @b.vector
            def _(v):
                for i, (m, n) in enumerate(sched):
                    wge(v, s_pe, i + 1)
                    last = v.scalar_tensor_tensor(
                        acc[:, m, n * 512:(n + 1) * 512], psd[i % 4][:],
                        sel_loc[:, m, e:e + 1],
                        acc[:, m, n * 512:(n + 1) * 512], ALU.mult, ALU.add)
                    p.inc(last, "apdv", 1)
        pstk.close()
    if stage <= 1:
        spill(p, "dbgacc", acc)
        accstk.close()
        p.stk.close()
        return p

    estk.close()
    y2_hbm = p.dram_tmp("y2_hbm", [DM, TOK])
    s3 = contextlib.ExitStack()
    s_bf = p.sb("s_bf", [128, DCH, TOK], BF16, s3)
    sq_bf = p.sb("sq_bf", [128, DCH, TOK], BF16, s3)
    sfp = [p.sb(f"sfp{i}", [128, TOK], F32, s3) for i in range(2)]
    xres = [p.sb(f"xres{i}", [128, TOK], F32, s3) for i in range(2)]
    _emit_tm2fm_res(p, "fo", acc, idf32, xin, s_hbm, s_bf, sq_bf, sfp, xres)
    accstk.close()
    emit_postln(p, "ln", s_hbm, s_bf, sq_bf, y2_hbm, gb=(ln2gb, ln2bb))
    _emit_add_ln(p, "ol", x5in, y2_hbm, s_hbm, s_bf, sq_bf, sfp, xres, xout)
    s3.close()
    p.stk.close()
    return p


def _emit_add_ln(p, tag, a_hbm, b_hbm, s_hbm, s_bf, sq_bf, sfp, xres, out_hbm):
    nc = p.nc
    s_a, s_b2 = p.snap(f"{tag}a"), p.snap(f"{tag}b")
    s_dv, s_vs = p.snap(f"{tag}dv"), p.snap(f"{tag}vs")
    with nc.Block() as b:
        @b.sync
        def _(sy):
            for c in range(DCH):
                if c >= 2:
                    wge(sy, s_dv, c - 1)
                p.dma(sy, xres[c % 2][:], a_hbm[c * 128:(c + 1) * 128, :],
                      f"{tag}a")

        @b.scalar
        def _(sc):
            for c in range(DCH):
                p.dma(sc, sfp[c % 2][:], b_hbm[c * 128:(c + 1) * 128, :],
                      f"{tag}b")
                if c >= 1:
                    wge(sc, s_vs, c)
                    sc.activation(sq_bf[:, c - 1, :], sfp[(c - 1) % 2][:],
                                  AF.Square)
                    p.dma(sc, s_hbm[(c - 1) * 128:c * 128, :],
                          sfp[(c - 1) % 2][:], f"{tag}st")
            wge(sc, s_vs, DCH)
            sc.activation(sq_bf[:, DCH - 1, :], sfp[(DCH - 1) % 2][:],
                          AF.Square)
            p.dma(sc, s_hbm[(DCH - 1) * 128:DCH * 128, :],
                  sfp[(DCH - 1) % 2][:], f"{tag}st")

        @b.vector
        def _(v):
            for c in range(DCH):
                wge(v, s_a, 16 * (c + 1))
                wge(v, s_b2, 16 * (c + 1))
                v.tensor_tensor(sfp[c % 2][:], sfp[c % 2][:], xres[c % 2][:],
                                ALU.add)
                last = v.tensor_copy(s_bf[:, c, :], sfp[c % 2][:])
                p.inc(last, f"{tag}vs", 1)
                p.inc(v.engine_nop(), f"{tag}dv", 1)

        @b.gpsimd
        def _(g):
            p.waitall(g, f"{tag}st")
    emit_postln(p, f"{tag}pl", s_hbm, s_bf, sq_bf, out_hbm)


def prep_l2fb(inputs, xps_percore, lgfull):
    w1 = np.asarray(inputs["L2_ffn_w1"])
    b1 = np.asarray(inputs["L2_ffn_b1"])
    w1aT = np.zeros((8, 4096, 1152), np.float32)
    for e in range(8):
        w1aT[e, :, :1024] = w1[e].T
        w1aT[e, :, 1024] = b1[e]
    base = {
        "lgf": lgfull,
        "w1aT": w1aT,
        "w2a": np.asarray(inputs["L2_ffn_w2"]),
        "b2a": np.asarray(inputs["L2_ffn_b2"]),
        "ln2g": _fmvec(inputs["L2_ffn_ln2_g"]),
        "ln2b": _fmvec(inputs["L2_ffn_ln2_b"]),
        "c_id": HC["ident"], "c_ones": HC["onescol"], "c_onesr": HC["onesrow"],
        "c_tin": HC["triu_inc"], "c_tex": HC["triu_exc"],
        "coreid": np.zeros((1, 1), np.float32),
    }
    maps = []
    for j in range(NCORE):
        m = dict(base)
        m["xin"] = xps_percore[j]
        maps.append(m)
    return maps


def prep_l2fc(inputs, xps_percore, selr, bcout, wcdv):
    base = {
        "wcd": wcdv,
        "bca": bcout,
        "ln2g": _fmvec(inputs["L2_ffn_ln2_g"]),
        "ln2b": _fmvec(inputs["L2_ffn_ln2_b"]),
        "c_id": HC["ident"], "c_idf": HC["ident"], "c_ones": HC["onescol"],
    }
    maps = []
    for j in range(NCORE):
        m = dict(base)
        m["xin"] = xps_percore[j]
        sl = np.zeros((128, NQT, 8), np.float32)
        for tl in range(NQT):
            for n16 in range(16):
                for bb in range(8):
                    pp = n16 * 8 + bb
                    rank = bb * 2048 + 256 * j + tl * 16 + n16
                    sl[pp, tl] = selr[rank]
        m["selin"] = np.ascontiguousarray(sl.reshape(128, NQT * 8))
        maps.append(m)
    return maps


def build_l3a(stage=99):
    p = P("l3a")
    nc = p.nc
    xin = p.dram_in("xin", [DM, TOK])
    wqk = p.dram_in("wqk", [DM, 2 * DM])
    wv = p.dram_in("wv", [DM, DM])
    wo = p.dram_in("wo", [DM, DM])
    bqk = p.dram_in("bqk", [128, 2 * DCH])
    bvr = p.dram_in("bvr", [1, DM])
    bo = p.dram_in("bo", [128, DCH])
    c_id = p.dram_in("c_id", [128, 128])
    c_ones = p.dram_in("c_ones", [128, 128])
    c_onesr = p.dram_in("c_onesr", [1, 128])
    c_wma = p.dram_in("c_wma", [128, 384])
    c_wbt = p.dram_in("c_wbt", [6 * 128, 512])
    xout = p.dram_out("xout", [DM, TOK])
    s_hbm = p.dram_tmp("s_hbm", [DM, TOK])

    p.idsb = p.sb("idsb", [128, 128], BF16)
    p.onesr = p.sb("onesr", [1, 128], BF16)
    p.onescol = p.sb("onescol", [128, 128], BF16)
    bqkb = p.sb("bqkb", [128, 2 * DCH], F32)
    bob = p.sb("bob", [128, DCH], F32)
    bvb = p.sb("bvb", [1, DM], BF16)
    onestok = p.sb("onestok", [1, 128], BF16)

    s1b = contextlib.ExitStack()
    wmab = p.sb("wmab", [128, 384], F32, s1b)
    wbtb = [p.sb(f"wbtb{r}", [128, 512], F32, s1b) for r in range(6)]
    qkb = p.sb("qkb", [128, 8, TOK], BF16, s1b)
    kkb = p.sb("kkb", [128, 8, TOK], BF16, s1b)
    vtm = p.sb("vtm", [128, NQT, DM], BF16, s1b)

    s1 = contextlib.ExitStack()
    xbf = p.sb("xbf", [128, DCH, TOK], BF16, s1)
    wb = p.sb("wb", [128, DCH, 2 * DM], BF16, s1)

    with nc.Block() as b:
        @b.gpsimd
        def _(g):
            cast_load(p, g, xbf, xin, DCH)
            cast_load(p, g, wb, wqk, DCH)
            p.dma(g, p.idsb[:], c_id)
            p.dma(g, p.onesr[:], c_onesr)
            p.dma(g, p.onescol[:], c_ones)
            p.dma(g, bvb[:], bvr)
            g.memset(onestok[:], 1.0)

        @b.sync
        def _(sy):
            p.dma(sy, wmab[:], c_wma)
            for r in range(6):
                p.dma(sy, wbtb[r][:], c_wbt[r * 128:(r + 1) * 128, :])
            p.dma(sy, bqkb[:], bqk)
            p.dma(sy, bob[:], bo)

    def qk_evac(v, m, n, ps):
        if m < 8:
            return v.tensor_scalar(qkb[:, m, n * 512:(n + 1) * 512], ps[:],
                                   bqkb[:, m:m + 1], 1.0 / 8.0,
                                   ALU.add, ALU.mult)
        return v.tensor_scalar(kkb[:, m - 8, n * 512:(n + 1) * 512], ps[:],
                               bqkb[:, m:m + 1], None, ALU.add)

    emit_linear_fm(p, "l3qk", wb, xbf, DCH, 16, qk_evac)

    with nc.Block() as b:
        @b.gpsimd
        def _(g):
            for k in range(DCH):
                p.dma(g, wb[:, k, 0:DM], wv[k * 128:(k + 1) * 128, :])
    pstk = contextlib.ExitStack()
    psv = [p.ps(f"l3vps{i}", [128, 512], F32, stk=pstk) for i in range(4)]
    s_pe, s_dv = p.snap("l3vpe"), p.snap("l3vdv")
    schedv = [(t, n) for t in range(NQT) for n in range(2)]
    with nc.Block() as b:
        @b.tensor
        def _(te):
            p.waitall(te)
            for i, (t, n) in enumerate(schedv):
                if i >= 4:
                    wge(te, s_dv, i - 3)
                for k in range(DCH):
                    te.matmul(psv[i % 4][:], xbf[:, k, t * 128:(t + 1) * 128],
                              wb[:, k, n * 512:(n + 1) * 512],
                              start=(k == 0), stop=False)
                mm = te.matmul(psv[i % 4][:], onestok[:],
                               bvb[:, n * 512:(n + 1) * 512],
                               start=False, stop=True)
                p.inc(mm, "l3vpe", 1)

        @b.vector
        def _(v):
            for i, (t, n) in enumerate(schedv):
                wge(v, s_pe, i + 1)
                last = v.tensor_copy(vtm[:, t, n * 512:(n + 1) * 512],
                                     psv[i % 4][:])
                p.inc(last, "l3vdv", 1)
    pstk.close()
    s1.close()

    s1c = contextlib.ExitStack()
    otm = p.sb("otm", [128, NQT, DM], BF16, s1c, side="right")
    p.att_side = "right"
    abks, btls = make_window_cbs(wmab, [w[:] for w in wbtb])
    emit_attention(p, "att", 16, qkb, kkb, vtm, otm, abks, btls, None, None,
                   stage=stage)
    if stage <= 7:
        spill(p, "dbg", otm)
        if hasattr(p, "_att_astk"):
            p._att_astk.close()
        s1c.close(); s1b.close(); p.stk.close()
        return p
    s1b.close()

    s3 = contextlib.ExitStack()
    obf = p.sb("obf", [128, DCH, TOK], BF16, s3)
    _emit_tm2fm(p, "o2f", otm, obf, DCH)
    s1c.close()

    wpb = p.sb("wpb", [128, DCH, DM], BF16, s3)
    s_bf = p.sb("s_bf", [128, DCH, TOK], BF16, s3)
    sq_bf = p.sb("sq_bf", [128, DCH, TOK], BF16, s3)
    sfp = [p.sb(f"sfp{i}", [128, TOK], F32, s3) for i in range(2)]
    xres = [p.sb(f"xres{i}", [128, TOK], F32, s3) for i in range(2)]
    with nc.Block() as b:
        @b.gpsimd
        def _(g):
            cast_load(p, g, wpb, wo, DCH)
    _emit_proj_res(p, "proj", wpb, obf, xin, s_hbm, s_bf, sq_bf, sfp, xres,
                   bias_sb=bob)
    emit_postln(p, "ln", s_hbm, s_bf, sq_bf, xout)
    s3.close()
    p.stk.close()
    return p


def prep_l3a(inputs, xfm_percore):
    wq = np.asarray(inputs["L3_attn_wq"])
    wk = np.asarray(inputs["L3_attn_wk"])
    base = {
        "wqk": np.ascontiguousarray(np.concatenate([wq, wk], axis=1)),
        "wv": np.asarray(inputs["L3_attn_wv"]),
        "wo": np.asarray(inputs["L3_attn_wo"]),
        "bqk": np.ascontiguousarray(np.concatenate(
            [_fmvec(inputs["L3_attn_bq"]), _fmvec(inputs["L3_attn_bk"])],
            axis=1)),
        "bvr": np.ascontiguousarray(
            np.asarray(inputs["L3_attn_bv"]).reshape(1, DM)),
        "bo": _fmvec(inputs["L3_attn_bo"]),
        "c_id": HC["ident"], "c_ones": HC["onescol"], "c_onesr": HC["onesrow"],
        "c_wma": HC["wma"], "c_wbt": HC["wbt"],
    }
    maps = []
    for b in range(NCORE):
        m = dict(base)
        m["xin"] = xfm_percore[b]
        maps.append(m)
    return maps


def build_l3f(stage=99):
    p = P("l3f")
    nc = p.nc
    xin = p.dram_in("xin", [DM, TOK])
    wupT = p.dram_in("wupT", [DM, 4096])
    wdnT = p.dram_in("wdnT", [4096, DM])
    c_ones = p.dram_in("c_ones", [128, 128])
    xout = p.dram_out("xout", [DM, TOK])
    s_hbm = p.dram_tmp("s_hbm", [DM, TOK])
    y_hbm = p.dram_tmp("y_hbm", [DM, TOK])
    wut_t = nc.dram_tensor("wut", [DM, 4096], BF16, kind="ExternalOutput")
    wut = wut_t.ap()
    wdt_t = nc.dram_tensor("wdt", [4096, DM], BF16, kind="ExternalOutput")
    wdt = wdt_t.ap()

    p.onescol = p.sb("onescol", [128, 128], BF16)
    ones32 = p.sb("ones32", [128, 128], F32)
    xnb = p.sb("xnb", [128, DCH, TOK], BF16)
    rb = p.sb("rb", [128, TOK], F32)
    rbb = p.sb("rbb", [128, TOK], BF16)
    xstk = contextlib.ExitStack()
    xbf = p.sb("xbf", [128, DCH, TOK], BF16, xstk)
    sqs = p.sb("sqs", [128, DCH, TOK], BF16, xstk)

    with nc.Block() as b:
        @b.gpsimd
        def _(g):
            cast_load(p, g, xbf, xin, DCH)
            p.dma(g, p.onescol[:], c_ones)
            g.memset(ones32[:], 1.0)

    def ternarize(tag, src_ap, dst_ap, nch, width):
        tstk = contextlib.ExitStack()
        wfp = [p.sb(f"{tag}w{i}", [128, width], F32, tstk) for i in range(2)]
        asum = p.sb(f"{tag}as", [128, nch], F32, tstk)
        a1 = p.sb(f"{tag}a1", [128, 1], F32, tstk)
        thr = p.sb(f"{tag}thr", [128, 1], F32, tstk)
        awt = p.sb(f"{tag}awt", [128, width], F32, tstk)
        sgn = p.sb(f"{tag}sg", [128, width], F32, tstk)
        wtb = p.sb(f"{tag}wt", [128, width], BF16, tstk)
        pstk = contextlib.ExitStack()
        psa = p.ps(f"{tag}psa", [128, 1], F32, stk=pstk)
        s_l, s_r = p.snap(f"{tag}l"), p.snap(f"{tag}r")
        with nc.Block() as b2:
            @b2.sync
            def _(sy):
                for k in range(nch):
                    if k >= 2:
                        wge(sy, s_r, k - 1)
                    p.dma(sy, wfp[k % 2][:], src_ap[k * 128:(k + 1) * 128, :],
                          f"{tag}l")

            @b2.vector
            def _(v):
                for k in range(nch):
                    wge(v, s_l, 16 * (k + 1))
                    last = v.reduce_sum(asum[:, k:k + 1], wfp[k % 2][:],
                                        axis=AX.X, apply_absolute_value=True)
                    p.inc(last, f"{tag}r", 1)

        with nc.Block() as b2:
            @b2.vector
            def _(v):
                p.inc(v.reduce_sum(a1[:], asum[:, 0:nch], axis=AX.X),
                      f"{tag}t1", 1)

            @b2.tensor
            def _(te):
                te.wait_ge(p.sem(f"{tag}t1"), p.cnt[f"{tag}t1"])
                p.inc(te.matmul(psa[:], ones32[:], a1[:], start=True,
                                stop=True), f"{tag}t2", 1)

            @b2.vector
            def _(v):
                v.wait_ge(p.sem(f"{tag}t2"), p.cnt[f"{tag}t2"])
                v.tensor_scalar_mul(thr[:], psa[:],
                                    0.5 / (nch * 128.0 * width))

        s_l2, s_v2, s_s2 = (p.snap(f"{tag}l2"), p.snap(f"{tag}v2"),
                            p.snap(f"{tag}s2_g"))
        s_a2, s_v2a = p.snap(f"{tag}a2"), p.snap(f"{tag}v2a")
        with nc.Block() as b2:
            @b2.sync
            def _(sy):
                for k in range(nch):
                    if k >= 2:
                        wge(sy, s_v2, k - 1)
                    p.dma(sy, wfp[k % 2][:], src_ap[k * 128:(k + 1) * 128, :],
                          f"{tag}l2")

            @b2.scalar
            def _(sc):
                for k in range(nch):
                    wge(sc, s_l2, 16 * (k + 1))
                    if k >= 1:
                        wge(sc, s_v2a, k)
                    sc.sign(sgn[:], wfp[k % 2][:])
                    last = sc.activation(awt[:], wfp[k % 2][:], AF.Abs)
                    p.inc(last, f"{tag}a2", 1)

            @b2.vector
            def _(v):
                for k in range(nch):
                    wge(v, s_a2, k + 1)
                    if k >= 1:
                        wge(v, s_s2, 16 * k)
                    p.chain(v, v.tensor_single_scalar(awt[:], awt[:],
                                                      thr[:, 0:1], ALU.is_gt))
                    last = v.tensor_tensor(wtb[:], sgn[:], awt[:], ALU.mult)
                    p.inc(last, f"{tag}v2", 1)
                    p.inc(v.engine_nop(), f"{tag}v2a", 1)

            @b2.gpsimd
            def _(g):
                for k in range(nch):
                    wge(g, s_v2, k + 1)
                    p.dma(g, dst_ap[k * 128:(k + 1) * 128, :], wtb[:],
                          f"{tag}s2")
        pstk.close()
        tstk.close()

    ternarize("tu", wupT, wut, DCH, 4096)
    ternarize("td", wdnT, wdt, 32, DM)

    pstk = contextlib.ExitStack()
    psS = p.ps("l3fS", [128, 2048], F32, stk=pstk)
    with nc.Block() as b:
        @b.scalar
        def _(sc):
            p.waitall(sc)
            p.inc(sc.activation(sqs.rearrange("p a b -> p (a b)"),
                                xbf.rearrange("p a b -> p (a b)"), AF.Square),
                  "l3fsq", 1)

        @b.tensor
        def _(te):
            te.wait_ge(p.sem("l3fsq"), p.cnt["l3fsq"])
            mm = None
            for c in range(DCH):
                for n in range(NT):
                    mm = te.matmul(psS[:, n * 512:(n + 1) * 512],
                                   p.onescol[:], sqs[:, c, n * 512:(n + 1) * 512],
                                   start=(c == 0), stop=(c == DCH - 1))
            p.inc(mm, "l3fst", 1)

        @b.vector
        def _(v):
            v.wait_ge(p.sem("l3fst"), p.cnt["l3fst"])
            p.inc(v.tensor_scalar_mul(rb[:], psS[:], 1.0 / DM), "l3fv1", 1)

        @b.scalar
        def _(sc):
            sc.wait_ge(p.sem("l3fv1"), p.cnt["l3fv1"])
            p.inc(sc.activation(rb[:], rb[:], AF.Sqrt), "l3fv2", 1)

        @b.vector
        def _(v):
            v.wait_ge(p.sem("l3fv2"), p.cnt["l3fv2"])
            p.chain(v, v.tensor_scalar_add(rb[:], rb[:], 1e-8))
            p.chain(v, v.reciprocal(rb[:], rb[:]))
            p.chain(v, v.tensor_copy(rbb[:], rb[:]))
            last = None
            for c in range(DCH):
                last = v.tensor_tensor(xnb[:, c, :], xbf[:, c, :], rbb[:],
                                       ALU.mult)
            p.inc(last, "l3fxn", 1)
    pstk.close()
    xstk.close()

    hstk = contextlib.ExitStack()
    hb = p.sb("hb", [128, 32, 1024], BF16, hstk)
    r2 = p.sb("r2", [128, 1024], F32, hstk)
    r2b = p.sb("r2b", [128, 1024], BF16, hstk)
    sqc = [p.sb(f"sqc{i}", [128, 1024], BF16, hstk) for i in range(2)]
    wsl = [p.sb(f"wsl{i}", [128, 32, 128], BF16, hstk) for i in range(2)]
    yh = [p.sb(f"yh{i}", [128, 1024], F32, hstk) for i in range(2)]
    for hf in range(2):
        t0 = hf * 1024
        pstk = contextlib.ExitStack()
        psl = [p.ps(f"u{hf}_{i}", [128, 512], F32, stk=pstk)
               for i in range(4)]
        s_wl, s_pe, s_dv = (p.snap("l3fuw_g"), p.snap("l3fupe"),
                            p.snap("l3fudv"))
        with nc.Block() as b:
            @b.gpsimd
            def _(g):
                g.wait_ge(p.sem("l3fxn"), p.cnt["l3fxn"])
                p.waitall(g, "tus2")
                p.waitall(g, "tds2")
                for m in range(32):
                    if m >= 2:
                        wge(g, s_pe, (m - 1) * 2)
                    p.dma(g, wsl[m % 2][:, 0:8, :],
                          wut[:, m * 128:(m + 1) * 128].rearrange(
                              "(a p) b -> p a b", p=128), "l3fuw")

            @b.tensor
            def _(te):
                i = 0
                for m in range(32):
                    wge(te, s_wl, 16 * (m + 1))
                    for n in range(2):
                        if i >= 4:
                            wge(te, s_dv, i - 3)
                        for k in range(DCH):
                            mm = te.matmul(
                                psl[i % 4][:],
                                wsl[m % 2][:, k, :],
                                xnb[:, k, t0 + n * 512:t0 + (n + 1) * 512],
                                start=(k == 0), stop=(k == DCH - 1))
                        p.inc(mm, "l3fupe", 1)
                        i += 1

            @b.scalar
            def _(sc):
                i = 0
                for m in range(32):
                    for n in range(2):
                        wge(sc, s_pe, i + 1)
                        last = sc.activation(
                            hb[:, m, n * 512:(n + 1) * 512], psl[i % 4][:],
                            AF.Gelu)
                        p.inc(last, "l3fudv", 1)
                        i += 1
        pstk.close()

        pstk = contextlib.ExitStack()
        psS2 = p.ps(f"l3fS2{hf}", [128, 1024], F32, stk=pstk)
        s_q2, s_s2m = p.snap("l3fq2"), p.snap("l3fs2m")
        with nc.Block() as b:
            @b.scalar
            def _(sc):
                for c in range(32):
                    if c >= 2:
                        wge(sc, s_s2m, c - 1)
                    last = sc.activation(sqc[c % 2][:], hb[:, c, :],
                                         AF.Square)
                    p.inc(last, "l3fq2", 1)

            @b.tensor
            def _(te):
                mm = None
                for c in range(32):
                    wge(te, s_q2, c + 1)
                    for n in range(2):
                        mm = te.matmul(psS2[:, n * 512:(n + 1) * 512],
                                       p.onescol[:],
                                       sqc[c % 2][:, n * 512:(n + 1) * 512],
                                       start=(c == 0), stop=(c == 31))
                    p.inc(mm, "l3fs2m", 1)

            @b.vector
            def _(v):
                v.wait_ge(p.sem("l3fs2m"), p.cnt["l3fs2m"])
                p.inc(v.tensor_scalar_mul(r2[:], psS2[:], 1.0 / 4096.0),
                      "l3fw1", 1)

            @b.scalar
            def _(sc):
                sc.wait_ge(p.sem("l3fw1"), p.cnt["l3fw1"])
                p.inc(sc.activation(r2[:], r2[:], AF.Sqrt), "l3fw2", 1)

            @b.vector
            def _(v):
                v.wait_ge(p.sem("l3fw2"), p.cnt["l3fw2"])
                p.chain(v, v.tensor_scalar_add(r2[:], r2[:], 1e-8))
                p.chain(v, v.reciprocal(r2[:], r2[:]))
                p.chain(v, v.tensor_copy(r2b[:], r2[:]))
                last = None
                for c in range(32):
                    last = v.tensor_tensor(hb[:, c, :], hb[:, c, :], r2b[:],
                                           ALU.mult)
                p.inc(last, "l3fhn", 1)
        pstk.close()

        pstk = contextlib.ExitStack()
        psd = [p.ps(f"d{hf}_{i}", [128, 512], F32, stk=pstk)
               for i in range(4)]
        s_wl2, s_pe2, s_dv2 = (p.snap("l3fdw_g"), p.snap("l3fdpe"),
                               p.snap("l3fddv"))
        s_ys = p.snap("l3fys")
        s_yv = p.snap("l3fyv")
        with nc.Block() as b:
            @b.gpsimd
            def _(g):
                g.wait_ge(p.sem("l3fhn"), p.cnt["l3fhn"])
                p.waitall(g, "tds2")
                for m in range(DCH):
                    if m >= 2:
                        wge(g, s_pe2, (m - 1) * 2)
                    p.dma(g, wsl[m % 2][:, :, :],
                          wdt[:, m * 128:(m + 1) * 128].rearrange(
                              "(a p) b -> p a b", p=128), "l3fdw")

            @b.tensor
            def _(te):
                i = 0
                for m in range(DCH):
                    wge(te, s_wl2, 16 * (m + 1))
                    for n in range(2):
                        if i >= 4:
                            wge(te, s_dv2, i - 3)
                        for k in range(32):
                            mm = te.matmul(
                                psd[i % 4][:],
                                wsl[m % 2][:, k, :],
                                hb[:, k, n * 512:(n + 1) * 512],
                                start=(k == 0), stop=(k == 31))
                        p.inc(mm, "l3fdpe", 1)
                        i += 1

            @b.vector
            def _(v):
                i = 0
                for m in range(DCH):
                    if m >= 2:
                        wge(v, s_ys, 16 * (m - 1))
                    for n in range(2):
                        wge(v, s_pe2, i + 1)
                        last = v.tensor_copy(yh[m % 2][:, n * 512:(n + 1) * 512],
                                             psd[i % 4][:])
                        p.inc(last, "l3fddv", 1)
                        i += 1
                    p.inc(v.engine_nop(), "l3fyv", 1)

            @b.scalar
            def _(sc):
                for m in range(DCH):
                    wge(sc, s_yv, m + 1)
                    p.dma(sc, y_hbm[m * 128:(m + 1) * 128, t0:t0 + 1024],
                          yh[m % 2][:], "l3fys")
        pstk.close()
    with nc.Block() as b:
        @b.gpsimd
        def _(g):
            p.waitall(g, "l3fys")
    hstk.close()

    s3 = contextlib.ExitStack()
    s_bf = p.sb("s_bf2", [128, DCH, TOK], BF16, s3)
    sq_bf = p.sb("sq_bf2", [128, DCH, TOK], BF16, s3)
    sfp = [p.sb(f"sfp{i}", [128, TOK], F32, s3) for i in range(2)]
    xres = [p.sb(f"xres{i}", [128, TOK], F32, s3) for i in range(2)]
    _emit_add_ln(p, "l3o", xin, y_hbm, s_hbm, s_bf, sq_bf, sfp, xres, xout)
    s3.close()
    p.stk.close()
    return p


def prep_l3f(inputs, xfm_percore):
    base = {
        "wupT": np.ascontiguousarray(np.asarray(inputs["L3_ffn_w_up"]).T),
        "wdnT": np.ascontiguousarray(np.asarray(inputs["L3_ffn_w_down"]).T),
        "c_ones": HC["onescol"],
    }
    maps = []
    for b in range(NCORE):
        m = dict(base)
        m["xin"] = xfm_percore[b]
        maps.append(m)
    return maps
